# version 20
# baseline (speedup 1.0000x reference)
import numpy as np
import ml_dtypes
from contextlib import ExitStack

import concourse.bass as bass
import concourse.mybir as mybir
from concourse.bass_utils import run_bass_kernel_spmd

F32 = mybir.dt.float32
BF16 = mybir.dt.bfloat16
AF = mybir.ActivationFunctionType
ALU = mybir.AluOpType
AX = mybir.AxisListType.X
bf16 = ml_dtypes.bfloat16

S = 2048
D = 4096
CF = 2048
NKC = 32
NFC = 16
TOWN = 1024
NEG = -1.0e9
RG = [[0, 1], [2, 3], [4, 5], [6, 7]]


class _Plan:
    STREAMS = ("CS", "HS", "HS2", "HS3", "HS4", "WA", "WB", "WV", "WV2", "GS", "SPA", "SPB", "VS", "VS2", "KL", "KL2", "KL3", "KL4",
               "QL", "OS", "OL", "OL2", "HA", "HB", "OT", "PE", "DV", "AC", "CC")

    def __init__(self):
        self.ops = {"sync": [], "gpsimd": [], "tensor": [], "vector": [], "scalar": []}
        self.cnt = {k: 0 for k in self.STREAMS}

    def emit(self, eng, fn, waits=(), inc=None, n=1, nops=1):
        w = []
        for item in waits:
            if not item:
                continue
            if isinstance(item, list):
                w += [x for x in item if x and x[1]]
            elif item[1]:
                w.append(item)
        if eng in ("vector", "scalar") and inc in (None, "DV" if eng == "vector" else "AC"):
            sem = "DV" if eng == "vector" else "AC"
            pre = self.cnt[sem]
            self.ops[eng].append((w, ("CHAIN", sem, pre, nops), nops, fn))
            self.cnt[sem] += nops
            return self.cnt[sem]
        self.ops[eng].append((w, inc, n, fn))
        if inc is not None:
            self.cnt[inc] += n
            return self.cnt[inc]
        return None

    def dma(self, eng, sem, dst, src, waits=()):
        return self.emit(eng, lambda e, d=dst, s=src: e.dma_start(out=d, in_=s),
                         waits, inc=sem, n=16)


def build_nc():
    nc = bass.Bass("TRN2", target_bir_lowering=False)
    di = lambda name, shape, dt: nc.dram_tensor(name, shape, dt, kind="ExternalInput").ap()

    hst = di("hst", [D, S], F32)
    hsto = di("hsto", [D, TOWN], F32)
    wq = di("wq", [NFC * 128, D], F32)
    wk = di("wk", [NFC * 128, D], F32)
    wv = di("wv", [8 * 128, 8192], F32)
    wout = di("wout", [32 * 128, D], F32)
    wbase = di("wbase", [32 * 128, D], F32)
    bb = di("bb", [128, 32], F32)
    wr = di("wr", [128, 256], F32)
    wla = di("wla", [128, D], F32)
    wlb = di("wlb", [128, D], F32)
    e2f = di("e2f", [8, 128], F32)
    cosd = di("cosd", [128, S], F32)
    sind = di("sind", [128, S], F32)
    identb = di("identb", [128, 128], BF16)
    identf = di("identf", [128, 128], F32)
    masktb = di("masktb", [128, 128], BF16)
    onesb = di("onesb", [128, 1], BF16)
    onesf = di("onesf", [1, 128], F32)
    epsv = di("epsv", [128, 1], F32)

    out = nc.dram_tensor("out", [TOWN, D], F32, kind="ExternalOutput").ap()

    q_s = nc.dram_tensor("q_s", [CF, S], BF16).ap()
    k_s = nc.dram_tensor("k_s", [CF, S], BF16).ap()
    v_s = nc.dram_tensor("v_s", [8 * 16 * 128, 256], BF16).ap()
    o_s = nc.dram_tensor("o_s", [4 * CF, 512], BF16)
    o_g = nc.dram_tensor("o_g", [4 * 2 * CF, 512], BF16)

    P = _Plan()
    C = P.cnt
    dyn_state = {}

    def dyn_col(base_ap, width):
        def mk(e, b=base_ap):
            if "toff" not in dyn_state:
                pid = e.partition_id()
                dyn_state["toff"] = (pid % 2) * TOWN
            return bass.AP(b.tensor, b.offset + dyn_state["toff"], b.ap)
        return mk

    with ExitStack() as top:
        et = top.enter_context
        sems = {k: et(nc.semaphore(k)) for k in _Plan.STREAMS}

        id_sb = et(nc.sbuf_tensor("id_sb", [128, 128], BF16))
        idf_sb = et(nc.sbuf_tensor("idf_sb", [128, 128], F32))
        maskb_sb = et(nc.sbuf_tensor("maskb_sb", [128, 128], BF16))
        ones_sb = et(nc.sbuf_tensor("ones_sb", [128, 1], BF16))
        onesf_sb = et(nc.sbuf_tensor("onesf_sb", [1, 128], F32))
        e2_sb = et(nc.sbuf_tensor("e2_sb", [8, 128], F32))
        bb_sb = et(nc.sbuf_tensor("bb_sb", [128, 32], F32))
        eps_sb = et(nc.sbuf_tensor("eps_sb", [128, 1], F32))

        for dst, src in ((id_sb[:], identb), (idf_sb[:], identf),
                         (maskb_sb[:], masktb),
                         (ones_sb[:], onesb), (onesf_sb[:], onesf), (e2_sb[:], e2f),
                         (bb_sb[:], bb), (eps_sb[:], epsv)):
            P.dma("sync", "CS", dst, src)

        with ExitStack() as pa:
            ea = pa.enter_context
            hs_sb = ea(nc.sbuf_tensor("hs_sb", [128, NKC * S], BF16))
            pa1 = ExitStack()
            e1a = pa1.enter_context
            cos_sb = e1a(nc.sbuf_tensor("cos_sb", [128, S], F32))
            sin_sb = e1a(nc.sbuf_tensor("sin_sb", [128, S], F32))
            wsl = [e1a(nc.sbuf_tensor(f"wsl{i}", [128, D], BF16)) for i in range(2)]
            sw_sb = e1a(nc.sbuf_tensor("sw_sb", [128, 512], F32))
            stage = [e1a(nc.sbuf_tensor(f"stage{i}", [128, S], BF16)) for i in range(2)]
            r1_sb = e1a(nc.sbuf_tensor("r1_sb", [128, 512], F32))
            r2_sb = e1a(nc.sbuf_tensor("r2_sb", [128, 512], F32))
            pk = [e1a(nc.psum_tensor(f"pk{i}", [128, 512], F32)) for i in range(2)]

            P.dma("sync", "CS", cos_sb[:], cosd)
            P.dma("sync", "CS", sin_sb[:], sind)
            cs_tot = C["CS"]

            hs3 = hst.rearrange("(kc p) t -> p kc t", p=128)
            hsb3 = hs_sb[:].rearrange("p (kc t) -> p kc t", kc=NKC)
            hs_streams = ("HS", "HS2", "HS3", "HS4")
            gp_hs_part = []
            for g in range(4):
                v = P.dma("gpsimd", hs_streams[g], hsb3[:, g * 8:(g + 1) * 8, :],
                          hs3[:, g * 8:(g + 1) * 8, :])
                gp_hs_part.append((hs_streams[g], v))

            slab_i = 0
            slab_lastuse = [0, 0]
            slab_ready = [0, 0]
            pg = 0
            pk_free = [[], []]
            stage_st = [0, 0]

            def load_slab(src_rows, dst=None):
                nonlocal slab_i
                buf = slab_i % 2
                tgt = wsl[buf][:] if dst is None else dst
                v = P.dma("gpsimd", "WA" if buf == 0 else "WB", tgt, src_rows,
                          waits=[("PE", slab_lastuse[buf])])
                slab_ready[buf] = v
                slab_i += 1
                return buf

            def mm_group(psum_ap, mms, waits, inc="PE"):
                def fn(e, ms=list(mms), pa_=psum_ap):
                    last = None
                    for k, (l, r) in enumerate(ms):
                        last = e.matmul(pa_, l, r, start=(k == 0), stop=(k == len(ms) - 1))
                    return last
                return P.emit("tensor", fn, waits, inc=inc)

            def proj_group(wbuf, ts, psum_buf, extra_waits=()):
                last = 0
                for g4 in range(4):
                    waits = [gp_hs_part[g4]]
                    if g4 == 0:
                        waits += [("WA" if wbuf == 0 else "WB", slab_ready[wbuf]),
                                  pk_free[psum_buf]] + list(extra_waits)
                    def fng(e, g4=g4, wbuf=wbuf, ts=ts, psum_buf=psum_buf):
                        lastmm = None
                        for kc in range(g4 * 8, g4 * 8 + 8):
                            lastmm = e.matmul(pk[psum_buf][:],
                                              wsl[wbuf][:, kc * 128:(kc + 1) * 128],
                                              hs_sb[:, kc * S + ts * 512: kc * S + ts * 512 + 512],
                                              start=(kc == g4 * 8 and g4 == 0),
                                              stop=(kc == 31 and g4 == 3),
                                              skip_group_check=True)
                        return lastmm
                    last = P.emit("tensor", fng, waits, inc="PE")
                return last

            PSWAP = [i ^ 1 for i in range(32)]

            def do_qk(w_main, dst_dram):
                nonlocal pg
                for fc in range(NFC):
                    h = fc // 2
                    mbuf = load_slab(w_main[fc * 128:(fc + 1) * 128, :])
                    sbuf_stage = stage[fc % 2]
                    pe_last = 0
                    ev_dv = ev_ac = 0
                    for ts in range(4):
                        pb = pg % 2
                        extra = ([("SPA" if fc % 2 == 0 else "SPB", stage_st[fc % 2]), ("CS", cs_tot)]
                                 if ts == 0 else [])
                        pe_last = proj_group(mbuf, ts, pb, extra)
                        if fc % 2 == 0:
                            hb = (h % 2) * 64
                            nb = 64 - hb
                            def rope(e, pb=pb, ts=ts, hb=hb, sg=sbuf_stage):
                                e.stream_shuffle(sw_sb[hb:hb + 64, :], pk[pb][hb:hb + 64, :], PSWAP)
                                e.tensor_tensor(r1_sb[hb:hb + 64, :], pk[pb][hb:hb + 64, :],
                                                cos_sb[hb:hb + 64, ts * 512:(ts + 1) * 512], ALU.mult)
                                e.tensor_tensor(r2_sb[hb:hb + 64, :], sw_sb[hb:hb + 64, :],
                                                sin_sb[hb:hb + 64, ts * 512:(ts + 1) * 512], ALU.mult)
                                return e.tensor_tensor(sg[hb:hb + 64, ts * 512:(ts + 1) * 512],
                                                       r1_sb[hb:hb + 64, :], r2_sb[hb:hb + 64, :], ALU.add)
                            ev_dv = P.emit("vector", rope, waits=[("PE", pe_last)], inc="DV", nops=4)
                            ev_ac = P.emit("scalar",
                                           lambda e, pb=pb, ts=ts, nb=nb, sg=sbuf_stage: e.copy(
                                               sg[nb:nb + 64, ts * 512:(ts + 1) * 512], pk[pb][nb:nb + 64, :]),
                                           waits=[("PE", pe_last)], inc="AC")
                            pk_free[pb] = [("AC", ev_ac), ("DV", ev_dv)]
                        else:
                            ev_ac = P.emit("scalar",
                                           lambda e, pb=pb, ts=ts, sg=sbuf_stage: e.copy(
                                               sg[:, ts * 512:(ts + 1) * 512], pk[pb][:]),
                                           waits=[("PE", pe_last)], inc="AC")
                            pk_free[pb] = [("AC", ev_ac)]
                        pg += 1
                    slab_lastuse[mbuf] = pe_last
                    stage_st[fc % 2] = P.dma(
                        "sync", "SPA" if fc % 2 == 0 else "SPB",
                        dst_dram[fc * 128:(fc + 1) * 128, :], sbuf_stage[:],
                        waits=[("DV", ev_dv), ("AC", ev_ac)])

            do_qk(wk, k_s)
            do_qk(wq, q_s)
            pa1.close()
            a1f = dict(C)
            a1bar = [("PE", a1f["PE"]), ("DV", a1f["DV"]), ("AC", a1f["AC"]),
                     ("SPA", a1f["SPA"]), ("SPB", a1f["SPB"])]

            wvsl = [ea(nc.sbuf_tensor(f"wvsl{i}", [128, 8192], BF16)) for i in range(2)]
            vst_sb = [ea(nc.sbuf_tensor(f"vst_sb{i}", [128, 16 * 256], BF16)) for i in range(2)]
            pv = [ea(nc.psum_tensor(f"pv{i}", [128, 256], F32)) for i in range(2)]
            v4 = v_s.rearrange("(dc tt p) n -> p dc tt n", p=128, tt=16)
            wv_lastuse = [0, 0]
            wv_ready = [0, 0]
            vst_st = [0, 0]
            pvg = 0
            pv_free = [(None, 0), (None, 0)]
            for dc in range(8):
                vb = dc % 2
                wv_ready[vb] = P.dma("gpsimd", "WV" if vb == 0 else "WV2", wvsl[vb][:],
                                     wv[dc * 128:(dc + 1) * 128, :],
                                     waits=([("PE", wv_lastuse[vb])] if dc >= 2 else a1bar))
                ev_last = 0
                for tt in range(16):
                    pb = pvg % 2
                    waits = [("WV" if vb == 0 else "WV2", wv_ready[vb]), pv_free[pb]]
                    if tt == 0:
                        waits.append(("VS" if vb == 0 else "VS2", vst_st[vb]))
                    if dc == 0 and tt == 0:
                        waits += [("DV", a1f["DV"]), ("AC", a1f["AC"])]
                    mms = [(hs_sb[:, kc * S + tt * 128: kc * S + tt * 128 + 128],
                            wvsl[vb][:, kc * 256:(kc + 1) * 256]) for kc in range(NKC)]
                    pe_v = mm_group(pv[pb][:], mms, list(gp_hs_part) + waits)
                    ev_last = P.emit("scalar",
                                     lambda e, pb=pb, tt=tt, vb=vb: e.copy(
                                         vst_sb[vb][:, tt * 256:(tt + 1) * 256], pv[pb][:]),
                                     waits=[("PE", pe_v)], inc="AC")
                    pv_free[pb] = ("AC", ev_last)
                    pvg += 1
                wv_lastuse[vb] = pe_v
                vst_st[vb] = P.dma("sync", "VS" if vb == 0 else "VS2", v4[:, dc, :, :],
                                   vst_sb[vb][:].rearrange("p (tt n) -> p tt n", tt=16),
                                   waits=[("AC", ev_last)])

        a_fin = dict(C)

        with ExitStack() as pb_:
            eb = pb_.enter_context
            it = 0
            k_sb = eb(nc.sbuf_tensor("k_sb", [128, NFC * S], BF16))
            v_sb = eb(nc.sbuf_tensor("v_sb", [128, 8 * 16 * 256], BF16))
            q_sb = eb(nc.sbuf_tensor("q_sb", [128, NFC * 512], BF16))
            p_sb = [eb(nc.sbuf_tensor(f"p_sb{i}", [128, S], BF16)) for i in range(4)]
            pt_sb = [eb(nc.sbuf_tensor(f"pt_sb{i}", [128, S], BF16)) for i in range(2)]
            ost_sb = eb(nc.sbuf_tensor("ost_sb", [128, NFC * 512], BF16))
            z_sb = eb(nc.sbuf_tensor("z_sb", [128, 8], F32))
            rz_sb = eb(nc.sbuf_tensor("rz_sb", [128, 8], F32))
            R6 = eb(nc.psum_tensor("R6", [128, 6 * 512], F32))
            av_ps = [eb(nc.psum_tensor(f"av_ps{i}", [128, 256], F32)) for i in range(2)]

            bar = [("PE", a_fin["PE"]), ("DV", a_fin["DV"]), ("AC", a_fin["AC"]),
                   ("SPA", a_fin["SPA"]), ("SPB", a_fin["SPB"]), ("VS", a_fin["VS"]),
                   ("VS2", a_fin["VS2"])]
            k3 = k_s.rearrange("(fc p) t -> p fc t", p=128)
            kb3 = k_sb[:].rearrange("p (fc t) -> p fc t", fc=NFC)
            v4r = v_s.rearrange("(dc tt p) n -> p dc tt n", p=128, tt=16)
            vb4 = v_sb[:].rearrange("p (dc tt n) -> p dc tt n", dc=8, tt=16)
            q3 = q_s.rearrange("(fc p) t -> p fc t", p=128)
            qb3 = q_sb[:].rearrange("p (fc t) -> p fc t", fc=NFC)
            kv_ready = []
            kl_streams = ("KL", "KL2", "KL3", "KL4")
            for g in range(4):
                cs0, cs1 = g * 512, (g + 1) * 512
                sy_kv = P.dma("sync", kl_streams[g], kb3[:, :, cs0:cs1],
                              k3[:, :, cs0:cs1], waits=bar if g == 0 else ())
                for dc in range(8):
                    sy_kv = P.dma("sync", kl_streams[g], vb4[:, dc, g * 4:(g + 1) * 4, :],
                                  v4r[:, dc, g * 4:(g + 1) * 4, :])
                kv_ready.append((kl_streams[g], sy_kv))

            bank_ev = [("DV", 0)] * 6

            def alloc(nb):
                while True:
                    for s0 in range(0, 7 - nb):
                        if all(bank_ev[i] is not None for i in range(s0, s0 + nb)):
                            waits = [bank_ev[i] for i in range(s0, s0 + nb)]
                            for i in range(s0, s0 + nb):
                                bank_ev[i] = None
                            return s0, waits
                    if pends:
                        back(pends.pop(0))
                    else:
                        raise RuntimeError("psum arena full")

            def release(s0, nb, ev):
                if ev is None:
                    return
                for i in range(s0, s0 + nb):
                    bank_ev[i] = ev

            pexp_free = [("PE", 0)] * 4
            zslot_free = [("DV", 0)] * 4
            pt_free = [("PE", 0)] * 2
            av_free = [("DV", 0)] * 2
            q_lastuse = 0
            sy_q = 0
            sy_ost = 0
            o4 = o_s.ap().rearrange("(sb fc p) t -> p sb fc t", p=128, sb=4)
            cc_vals = []
            pends = []

            def sc_part(h, qt, c0, ew, bank0, extra_waits, add_mask):
                pe_sc = 0
                off = c0
                bk = bank0
                while off < c0 + ew:
                    w = min(512, c0 + ew - off)
                    mms = [(q_sb[:, (2 * h + dk) * 512 + qt * 128: (2 * h + dk) * 512 + qt * 128 + 128],
                            k_sb[:, (2 * h + dk) * S + off: (2 * h + dk) * S + off + w])
                           for dk in (0, 1)]
                    last_chunk = (off + w >= c0 + ew)
                    def fn(e, ms=list(mms), bk=bk, w=w, last=last_chunk and add_mask):
                        out = R6[:, bk * 512: bk * 512 + w]
                        lastmm = None
                        n = len(ms)
                        for kx, (l, r) in enumerate(ms):
                            lastmm = e.matmul(out, l, r, start=(kx == 0),
                                              stop=(kx == n - 1 and not last),
                                              skip_group_check=True)
                        if last:
                            lastmm = e.matmul(R6[:, bk * 512 + w - 128: bk * 512 + w],
                                              id_sb[:], maskb_sb[:], start=False, stop=True,
                                              skip_group_check=True)
                        return lastmm
                    pe_sc = P.emit("tensor", fn, extra_waits if off == c0 else (), inc="PE")
                    off += w
                    bk += 1
                return pe_sc

            def front(it, sb, qt, h):
                i_glob = sb * 4 + qt
                ext = (i_glob + 1) * 128
                nsub = i_glob + 1
                pr = it % 4
                zs = it % 4
                extA = min(ext, 1024)
                extB = ext - extA
                nbA = max(2, (extA + 511) // 512)
                a0, wA = alloc(nbA)
                pe_scA = sc_part(h, qt, 0, extA, a0,
                                 wA + [("QL", sy_q), ("CS", cs_tot),
                                       kv_ready[(extA - 1) // 512]], extB == 0)
                ac_expA = P.emit("scalar",
                                 lambda e, a0=a0, extA=extA, pr=pr, zs=zs: e.activation(
                                     p_sb[pr][:, 0:extA], R6[:, a0 * 512: a0 * 512 + extA],
                                     AF.Exp, scale=0.0625,
                                     accum_out=z_sb[:, zs * 2: zs * 2 + 1]),
                                 waits=[("PE", pe_scA), pexp_free[pr], zslot_free[zs]],
                                 inc="AC")
                if extB:
                    release(a0, nbA, ("AC", ac_expA))
                    nbB = max(2, (extB + 511) // 512)
                    b0, wB = alloc(nbB)
                    pe_scB = sc_part(h, qt, 1024, extB, b0,
                                     wB + [kv_ready[(ext - 1) // 512]], True)
                    P.emit("scalar",
                           lambda e, b0=b0, extB=extB, pr=pr, zs=zs: e.activation(
                               p_sb[pr][:, 1024:1024 + extB],
                               R6[:, b0 * 512: b0 * 512 + extB],
                               AF.Exp, scale=0.0625,
                               accum_out=z_sb[:, zs * 2 + 1: zs * 2 + 2]),
                           waits=[("PE", pe_scB)], inc="AC")
                    P.emit("scalar",
                           lambda e, zs=zs: e.add(z_sb[:, zs * 2: zs * 2 + 1],
                                                  z_sb[:, zs * 2: zs * 2 + 1],
                                                  z_sb[:, zs * 2 + 1: zs * 2 + 2]),
                           inc="AC")
                P.emit("scalar",
                       lambda e, zs=zs: e.activation(rz_sb[:, zs:zs + 1],
                                                     z_sb[:, zs * 2: zs * 2 + 1], AF.Ln),
                       inc="AC")
                P.emit("scalar",
                       lambda e, zs=zs: e.activation(rz_sb[:, 4 + zs:5 + zs],
                                                     rz_sb[:, zs:zs + 1], AF.Exp, scale=-1.0),
                       inc="AC")
                ac_n = P.emit("scalar",
                              lambda e, ext=ext, pr=pr, zs=zs: e.activation(
                                  p_sb[pr][:, 0:ext], p_sb[pr][:, 0:ext], AF.Identity,
                                  scale=rz_sb[:, 4 + zs:5 + zs]),
                              inc="AC")
                if extB:
                    release(b0, nbB, None)
                    win0, win_nb = b0, nbB
                else:
                    release(a0, nbA, None)
                    win0, win_nb = a0, nbA
                zslot_free[zs] = ("AC", ac_n)
                return dict(it=it, sb=sb, qt=qt, h=h, nsub=nsub, pr=pr,
                            win0=win0, win_nb=win_nb, dv_dg=ac_n)

            def back(st):
                nonlocal sy_ost
                bit = st["it"]; sb = st["sb"]; qt = st["qt"]; h = st["h"]
                nsub = st["nsub"]; pr = st["pr"]
                win0 = st["win0"]; win_nb = st["win_nb"]; dv_dg = st["dv_dg"]
                ngr = (nsub + 3) // 4
                pb = bit % 2
                win_ev = [None] * win_nb
                ac_pt = 0
                pe_tr_last = 0
                for g in range(ngr):
                    wb = win0 + (g % win_nb)
                    s0 = g * 4
                    s1 = min(nsub, s0 + 4)
                    def trg(e, s0=s0, s1=s1, wb=wb, pr=pr, bit=bit):
                        last = None
                        for sx in range(s0, s1):
                            last = e.matmul(R6[:, wb * 512 + (sx - s0) * 128: wb * 512 + (sx - s0) * 128 + 128],
                                            p_sb[pr][:, sx * 128:(sx + 1) * 128],
                                            id_sb[:], start=True, stop=True)
                        return last
                    waits = [("AC", dv_dg)]
                    if win_ev[g % win_nb] is not None:
                        waits.append(win_ev[g % win_nb])
                    pe_tr_last = P.emit("tensor", trg, waits, inc="PE")
                    nw = (s1 - s0) * 128
                    ac_pt = P.emit("vector",
                                   lambda e, s0=s0, nw=nw, wb=wb, pb=pb: e.tensor_copy(
                                       pt_sb[pb][:, s0 * 128: s0 * 128 + nw],
                                       R6[:, wb * 512: wb * 512 + nw]),
                                   waits=[("PE", pe_tr_last)], inc="DV")
                    win_ev[g % win_nb] = ("DV", ac_pt)
                pexp_free[pr] = ("PE", pe_tr_last)
                for wq_ in range(win_nb):
                    if win_ev[wq_] is not None:
                        release(win0 + wq_, 1, win_ev[wq_])
                    else:
                        release(win0 + wq_, 1, ("AC", ac_pt))

                def avg(e, nsub=nsub, h=h, pb=pb):
                    last = None
                    for dv in (0, 1):
                        for sx in range(nsub):
                            last = e.matmul(av_ps[pb][:, dv * 128: dv * 128 + 128],
                                            v_sb[:, h * 4096 + sx * 256 + dv * 128: h * 4096 + sx * 256 + dv * 128 + 128],
                                            pt_sb[pb][:, sx * 128:(sx + 1) * 128],
                                            start=(sx == 0), stop=(sx == nsub - 1))
                    return last
                pe_av = P.emit("tensor", avg,
                               waits=[w for w in win_ev if w] + [av_free[pb],
                                      kv_ready[(nsub - 1) // 4]], inc="PE")
                pt_free[pb] = ("PE", pe_av)

                owaits = [("PE", pe_av)]
                if bit % 32 == 0:
                    owaits.append(("OS", sy_ost))
                dv_oevac = P.emit(
                    "vector",
                    lambda e, h=h, qt=qt, pb=pb: e.tensor_copy(
                        ost_sb[:].rearrange("p (fc t) -> p fc t", fc=NFC)[:, 2 * h:2 * h + 2, qt * 128:(qt + 1) * 128],
                        av_ps[pb][:].rearrange("p (dv t) -> p dv t", dv=2)),
                    waits=owaits, inc="DV")
                av_free[pb] = ("DV", dv_oevac)
                if bit % 32 == 31:
                    sy_ost = P.dma("sync", "OS",
                                   o4[:, sb, :, :],
                                   ost_sb[:].rearrange("p (fc t) -> p fc t", fc=NFC),
                                   waits=[("DV", dv_oevac)])
                    cc_vals.append(P.emit(
                        "gpsimd",
                        lambda e, sb=sb: e.collective_compute(
                            "AllGather", ALU.bypass, replica_groups=RG,
                            ins=[o_s.ap()[sb * CF:(sb + 1) * CF, :].opt()],
                            outs=[o_g.ap()[sb * 2 * CF:(sb + 1) * 2 * CF, :].opt()]),
                        waits=[("OS", sy_ost)], inc="CC"))

            for sb in range(4):
                sy_q = P.dma("sync", "QL", qb3, q3[:, :, sb * 512:(sb + 1) * 512],
                             waits=[("PE", q_lastuse)] + (bar if sb == 0 else []))
                for qt in range(4):
                    for h in range(8):
                        pends.append(front(it, sb, qt, h))
                        if qt == 3 and h == 7:
                            q_lastuse = C["PE"]
                        if len(pends) > 3:
                            back(pends.pop(0))
                        it += 1
            while pends:
                back(pends.pop(0))

        b_fin = dict(C)

        cc_v = C["CC"]

        with ExitStack() as pc:
            ec = pc.enter_context
            x1_sb = ec(nc.sbuf_tensor("x1_sb", [128, 32 * TOWN], BF16))
            wsl2 = [ec(nc.sbuf_tensor(f"wsl2_{i}", [128, D], BF16)) for i in range(2)]
            slab2_ready = [0, 0]
            slab2_lastuse = [b_fin["PE"], b_fin["PE"]]
            slab2_i = 0

            def load_slab2(src_rows):
                nonlocal slab2_i
                buf = slab2_i % 2
                v = P.dma("gpsimd", "WA" if buf == 0 else "WB", wsl2[buf][:], src_rows,
                          waits=[("PE", slab2_lastuse[buf])])
                slab2_ready[buf] = v
                slab2_i += 1
                return buf

            cbar = [("PE", b_fin["PE"]), ("DV", b_fin["DV"]), ("AC", b_fin["AC"])]

            wla_sb = ec(nc.sbuf_tensor("wla_sb", [128, D], BF16))
            wlb_sb = ec(nc.sbuf_tensor("wlb_sb", [128, D], BF16))
            wr_sb = ec(nc.sbuf_tensor("wr_sb", [128, 256], BF16))
            gp_small = 0
            for dst, src in ((wla_sb[:], wla), (wlb_sb[:], wlb), (wr_sb[:], wr)):
                gp_small = P.dma("gpsimd", "GS", dst, src, waits=[("PE", b_fin["PE"])])
            lg_sb = ec(nc.sbuf_tensor("lg_sb", [8, TOWN], F32))
            h_sb = ec(nc.sbuf_tensor("h_sb", [128, TOWN], BF16))

            with ExitStack() as c1:
                e1 = c1.enter_context
                oo_sb = e1(nc.sbuf_tensor("oo_sb", [128, 32 * TOWN], BF16))
                hso = [e1(nc.sbuf_tensor(f"hso{i}", [128, TOWN], F32)) for i in range(2)]
                lt_sb = e1(nc.sbuf_tensor("lt_sb", [128, TOWN], F32))
                sq_sb = [e1(nc.sbuf_tensor(f"sq_sb{i}", [128, TOWN], BF16)) for i in range(2)]
                m_sb = e1(nc.sbuf_tensor("m_sb", [1, TOWN], F32))
                ex2_sb = e1(nc.sbuf_tensor("ex2_sb", [1, TOWN], F32))
                var_sb = e1(nc.sbuf_tensor("var_sb", [1, TOWN], F32))
                rs_sb = e1(nc.sbuf_tensor("rs_sb", [1, TOWN], F32))
                sd_sb = ex2_sb
                nmrs_sb = var_sb
                po = [e1(nc.psum_tensor(f"po{i}", [128, TOWN], F32)) for i in range(2)]
                st_scope = ExitStack()
                st_ps = st_scope.enter_context(nc.psum_tensor("st_ps", [1, 2048], F32))

                os4 = o_s.ap().rearrange("(sb fc p) t -> p sb fc t", p=128, sb=4)
                og4 = o_g.ap().rearrange("(sb fc p) t -> p sb fc t", p=128, sb=4)
                oo3 = oo_sb[:].rearrange("p (fc t) -> p fc t", t=TOWN)
                sy_own = 0
                for sbl in range(2):
                    base = os4[:, sbl, :, :]
                    def fn_own(e, b=base):
                        if "soff" not in dyn_state:
                            pid = e.partition_id()
                            dyn_state["soff"] = (pid % 2) * (2 * CF * 512)
                            dyn_state["goff"] = (pid % 2) * (2 * 2 * CF * 512) + \
                                ((pid + 1) % 2) * (CF * 512)
                        return bass.AP(b.tensor, b.offset + dyn_state["soff"], b.ap)
                    sy_own = P.emit("sync",
                                    lambda e, fn=fn_own, sbl=sbl: e.dma_start(
                                        out=oo3[:, 0:16, sbl * 512:(sbl + 1) * 512],
                                        in_=fn(e)),
                                    waits=([("OS", b_fin["OS"])] + cbar) if sbl == 0 else (),
                                    inc="OL", n=16)
                sy_oo = 0
                for sbl in range(2):
                    base = og4[:, sbl, 0:16, :]
                    def fn_par(e, b=base):
                        return bass.AP(b.tensor, b.offset + dyn_state["goff"], b.ap)
                    sy_oo = P.emit("sync",
                                   lambda e, fn=fn_par, sbl=sbl: e.dma_start(
                                       out=oo3[:, 16:32, sbl * 512:(sbl + 1) * 512],
                                       in_=fn(e)),
                                   waits=[("CC", cc_v)] if sbl == 0 else (),
                                   inc="OL2", n=16)

                po_free = [(None, 0), (None, 0)]
                sq_free = [(None, 0), (None, 0)]
                hso_free = [(None, 0), (None, 0)]
                dv_x1 = 0
                pe_stat = 0
                stat_pend = None

                def emit_stats(oc, dvv, acv):
                    nonlocal pe_stat
                    def stats(e, oc=oc):
                        last = None
                        for hf in (0, 1):
                            last = e.matmul(st_ps[:, hf * 512:(hf + 1) * 512], ones_sb[:],
                                            x1_sb[:, oc * TOWN + hf * 512: oc * TOWN + hf * 512 + 512],
                                            start=(oc == 0), stop=(oc == 31), skip_group_check=True)
                        for hf in (0, 1):
                            last = e.matmul(st_ps[:, 1024 + hf * 512: 1024 + hf * 512 + 512], ones_sb[:],
                                            sq_sb[oc % 2][:, hf * 512:(hf + 1) * 512],
                                            start=(oc == 0), stop=(oc == 31), skip_group_check=True)
                        return last
                    pe_stat = P.emit("tensor", stats,
                                     waits=[("DV", dvv), ("AC", acv), ("CS", cs_tot)], inc="PE")
                    sq_free[oc % 2] = ("PE", pe_stat)

                for oc in range(32):
                    wb = load_slab2(wout[oc * 128:(oc + 1) * 128, :])
                    sy_h = P.dma("sync", "HA" if oc % 2 == 0 else "HB", hso[oc % 2][:],
                                 hsto[oc * 128:(oc + 1) * 128, :],
                                 waits=[hso_free[oc % 2]])
                    pe_po = 0
                    for hf in (0, 1):
                        mms = [(wsl2[wb][:, kc * 128:(kc + 1) * 128],
                                oo_sb[:, kc * TOWN + hf * 512: kc * TOWN + hf * 512 + 512])
                               for kc in range(16)]
                        waits = ([("WA" if wb == 0 else "WB", slab2_ready[wb]),
                                  ("OL", sy_own), po_free[oc % 2]] if hf == 0 else [])
                        def fno(e, ms=list(mms), oc=oc, hf=hf):
                            last = None
                            for kx, (l, r) in enumerate(ms):
                                last = e.matmul(po[oc % 2][:, hf * 512:(hf + 1) * 512], l, r,
                                                start=(kx == 0), stop=False,
                                                skip_group_check=True)
                            return last
                        P.emit("tensor", fno, waits, inc="PE")
                        mms2 = [(wsl2[wb][:, kc * 128:(kc + 1) * 128],
                                 oo_sb[:, kc * TOWN + hf * 512: kc * TOWN + hf * 512 + 512])
                                for kc in range(16, 32)]
                        def fnp(e, ms=list(mms2), oc=oc, hf=hf):
                            last = None
                            for kx, (l, r) in enumerate(ms):
                                last = e.matmul(po[oc % 2][:, hf * 512:(hf + 1) * 512], l, r,
                                                start=False, stop=(kx == len(ms) - 1),
                                                skip_group_check=True)
                            return last
                        pe_po = P.emit("tensor", fnp, [("OL2", sy_oo)], inc="PE")
                    slab2_lastuse[wb] = pe_po
                    dv_x1 = P.emit("vector",
                                   lambda e, oc=oc: e.tensor_tensor(
                                       x1_sb[:, oc * TOWN:(oc + 1) * TOWN],
                                       po[oc % 2][:], hso[oc % 2][:], ALU.add),
                                   waits=[("PE", pe_po), ("HA" if oc % 2 == 0 else "HB", sy_h)], inc="DV")
                    po_free[oc % 2] = ("DV", dv_x1)
                    hso_free[oc % 2] = ("DV", dv_x1)
                    ac_sq = P.emit("scalar",
                                   lambda e, oc=oc: e.activation(
                                       sq_sb[oc % 2][:], x1_sb[:, oc * TOWN:(oc + 1) * TOWN],
                                       AF.Square),
                                   waits=[("DV", dv_x1), sq_free[oc % 2]], inc="AC")
                    if stat_pend is not None:
                        emit_stats(*stat_pend)
                    stat_pend = (oc, dv_x1, ac_sq)
                emit_stats(*stat_pend)

                def fin1(e):
                    e.tensor_scalar(m_sb[:], st_ps[:, 0:1024], 1.0 / D, None, ALU.mult)
                    e.tensor_scalar(ex2_sb[:], st_ps[:, 1024:2048], 1.0 / D, None, ALU.mult)
                    e.tensor_tensor(var_sb[:], m_sb[:], m_sb[:], ALU.mult)
                    return e.tensor_tensor(var_sb[:], ex2_sb[:], var_sb[:], ALU.subtract)
                dv_f1 = P.emit("vector", fin1, waits=[("PE", pe_stat)], inc="DV", nops=4)
                st_scope.close()
                lg_ps = e1(nc.psum_tensor("lg_ps", [8, TOWN], F32))
                h_ps = e1(nc.psum_tensor("h_ps", [128, TOWN], F32))
                ac_f1 = P.emit("scalar",
                               lambda e: e.activation(sd_sb[:], var_sb[:], AF.Sqrt, bias=eps_sb[0:1, 0:1]),
                               waits=[("DV", dv_f1)], inc="AC")
                def fin2(e):
                    e.reciprocal(rs_sb[:], sd_sb[:])
                    e.tensor_tensor(nmrs_sb[:], m_sb[:], rs_sb[:], ALU.mult)
                    return e.tensor_scalar(nmrs_sb[:], nmrs_sb[:], -1.0, None, ALU.mult)
                dv_f2 = P.emit("vector", fin2, waits=[("AC", ac_f1)], inc="DV", nops=3)
                def bcast(e):
                    last = None
                    for hf in (0, 1):
                        last = e.matmul(po[0][:, hf * 512:(hf + 1) * 512], onesf_sb[:],
                                        rs_sb[:, hf * 512:(hf + 1) * 512], start=True, stop=True)
                        last = e.matmul(po[1][:, hf * 512:(hf + 1) * 512], onesf_sb[:],
                                        nmrs_sb[:, hf * 512:(hf + 1) * 512], start=True, stop=True)
                    return last
                pe_bc = P.emit("tensor", bcast, waits=[("DV", dv_f2), ("DV", dv_x1)], inc="PE")
                dv_ap = 0
                pe_rt = 0
                dv_ap_at = []
                for oc in range(32):
                    def apply1(e, oc=oc):
                        e.tensor_tensor(lt_sb[:], x1_sb[:, oc * TOWN:(oc + 1) * TOWN],
                                        po[1][:], ALU.add)
                        return e.tensor_tensor(x1_sb[:, oc * TOWN:(oc + 1) * TOWN],
                                               lt_sb[:], po[0][:], ALU.mult)
                    dv_ap = P.emit("vector", apply1,
                                   waits=[("PE", max(pe_bc, pe_stat))], inc="DV", nops=2)
                    dv_ap_at.append(dv_ap)
                    def rtl(e, oc=oc):
                        last = None
                        for hf in (0, 1):
                            last = e.matmul(lg_ps[:, hf * 512:(hf + 1) * 512],
                                            wr_sb[:, oc * 8:(oc + 1) * 8],
                                            x1_sb[:, oc * TOWN + hf * 512: oc * TOWN + hf * 512 + 512],
                                            start=(oc == 0), stop=(oc == 31), skip_group_check=True)
                        for hf in (0, 1):
                            last = e.matmul(h_ps[:, hf * 512:(hf + 1) * 512],
                                            wla_sb[:, oc * 128:(oc + 1) * 128],
                                            x1_sb[:, oc * TOWN + hf * 512: oc * TOWN + hf * 512 + 512],
                                            start=(oc == 0), stop=(oc == 31), skip_group_check=True)
                        return last
                    pe_rt = P.emit("tensor", rtl,
                                   waits=[("DV", dv_ap), ("GS", gp_small)], inc="PE")
                dv_lg = P.emit("vector", lambda e: e.tensor_copy(lg_sb[:], lg_ps[:]),
                               waits=[("PE", pe_rt)], inc="DV")
                dv_h = P.emit("vector", lambda e: e.tensor_copy(h_sb[:], h_ps[:]),
                              waits=[("PE", pe_rt)], inc="DV")

            hw_sb = ec(nc.sbuf_tensor("hw_sb", [128, TOWN], BF16))
            lgt_sb = ec(nc.sbuf_tensor("lgt_sb", [128, 64], F32))
            oh1_sb = ec(nc.sbuf_tensor("oh1_sb", [128, 64], F32))
            oh2_sb = ec(nc.sbuf_tensor("oh2_sb", [128, 64], F32))
            l2_sb = ec(nc.sbuf_tensor("l2_sb", [128, 64], F32))
            wtok_sb = ec(nc.sbuf_tensor("wtok_sb", [128, 64], F32))
            wtk2_sb = ec(nc.sbuf_tensor("wtk2_sb", [128, 8], F32))
            tkw_sb = ec(nc.sbuf_tensor("tkw_sb", [128, 8], F32))
            tk2_sb = ec(nc.sbuf_tensor("tk2_sb", [128, 32], F32))
            wf_sb = ec(nc.sbuf_tensor("wf_sb", [8, TOWN], F32))
            x2tok_sb = ec(nc.sbuf_tensor("x2tok_sb", [128, 8 * D], BF16))
            x2c_sb = [ec(nc.sbuf_tensor(f"x2c{i}", [128, TOWN], F32)) for i in range(2)]
            s1p_sb = ec(nc.sbuf_tensor("s1p_sb", [128, 8 * 32], F32))
            s2p_sb = ec(nc.sbuf_tensor("s2p_sb", [128, 8 * 32], F32))
            s1_sb = ec(nc.sbuf_tensor("s1_sb", [128, 8], F32))
            s2_sb = ec(nc.sbuf_tensor("s2_sb", [128, 8], F32))
            mean2_sb = ec(nc.sbuf_tensor("mean2_sb", [128, 8], F32))
            var2_sb = ec(nc.sbuf_tensor("var2_sb", [128, 8], F32))
            rs2_sb = ec(nc.sbuf_tensor("rs2_sb", [128, 8], F32))
            nm2_sb = ec(nc.sbuf_tensor("nm2_sb", [128, 8], F32))
            ofin_sb = ec(nc.sbuf_tensor("ofin_sb", [128, D], F32))

            with ExitStack() as c2a:
                e2a = c2a.enter_context
                t8_ps = e2a(nc.psum_tensor("t8_ps", [128, 64], F32))
                wf_ps = e2a(nc.psum_tensor("wf_ps", [8, TOWN], F32))
                wexp_ps = e2a(nc.psum_tensor("wexp_ps", [128, TOWN], F32))

                def t8s(e):
                    last = None
                    for tt in range(8):
                        last = e.transpose(t8_ps[:, tt * 8:(tt + 1) * 8],
                                           lg_sb[:, tt * 128:(tt + 1) * 128], idf_sb[0:8, 0:8])
                    return last
                pe_t8 = P.emit("tensor", t8s, waits=[("DV", dv_lg)], inc="PE")

                def topk1(e):
                    last = None
                    for tt in range(8):
                        sl = lgt_sb[:, tt * 8:(tt + 1) * 8]
                        e.tensor_copy(sl, t8_ps[:, tt * 8:(tt + 1) * 8])
                        e.reduce_max(tk2_sb[:, tt:tt + 1], sl, axis=AX)
                        e.tensor_scalar(oh1_sb[:, tt * 8:(tt + 1) * 8], sl,
                                        tk2_sb[:, tt:tt + 1], None, ALU.is_equal)
                        e.scalar_tensor_tensor(l2_sb[:, tt * 8:(tt + 1) * 8],
                                               oh1_sb[:, tt * 8:(tt + 1) * 8], -1.0e30, sl,
                                               ALU.mult, ALU.add)
                        e.reduce_max(tk2_sb[:, 8 + tt:9 + tt], l2_sb[:, tt * 8:(tt + 1) * 8], axis=AX)
                        e.tensor_scalar(oh2_sb[:, tt * 8:(tt + 1) * 8],
                                        l2_sb[:, tt * 8:(tt + 1) * 8],
                                        tk2_sb[:, 8 + tt:9 + tt], None, ALU.is_equal)
                        last = e.tensor_tensor(tkw_sb[:, tt:tt + 1], tk2_sb[:, tt:tt + 1],
                                               tk2_sb[:, 8 + tt:9 + tt], ALU.subtract)
                    return last
                dv_t1 = P.emit("vector", topk1, waits=[("PE", pe_t8)], inc="DV", nops=56)
                ac_e = P.emit("scalar",
                              lambda e: e.activation(tk2_sb[:, 16:24], tkw_sb[:], AF.Exp,
                                                     scale=-1.0),
                              waits=[("DV", dv_t1)], inc="AC")
                def topk2(e):
                    last = None
                    for tt in range(8):
                        e.tensor_scalar(tk2_sb[:, 24 + tt:25 + tt], tk2_sb[:, 16 + tt:17 + tt],
                                        1.0, None, ALU.add)
                        e.reciprocal(tkw_sb[:, tt:tt + 1], tk2_sb[:, 24 + tt:25 + tt])
                        e.tensor_scalar(wtk2_sb[:, 0:1], tkw_sb[:, tt:tt + 1], 2.0, None, ALU.mult)
                        e.tensor_scalar(wtk2_sb[:, 1:2], tkw_sb[:, tt:tt + 1], -2.0, 2.0,
                                        ALU.mult, ALU.add)
                        e.tensor_scalar(wtok_sb[:, tt * 8:(tt + 1) * 8],
                                        oh1_sb[:, tt * 8:(tt + 1) * 8],
                                        wtk2_sb[:, 0:1], None, ALU.mult)
                        e.tensor_scalar(l2_sb[:, tt * 8:(tt + 1) * 8],
                                        oh2_sb[:, tt * 8:(tt + 1) * 8],
                                        wtk2_sb[:, 1:2], None, ALU.mult)
                        last = e.tensor_tensor(wtok_sb[:, tt * 8:(tt + 1) * 8],
                                               wtok_sb[:, tt * 8:(tt + 1) * 8],
                                               l2_sb[:, tt * 8:(tt + 1) * 8], ALU.add)
                    return last
                dv_t2 = P.emit("vector", topk2, waits=[("AC", ac_e)], inc="DV", nops=56)
                def wfs(e):
                    last = None
                    for tt in range(8):
                        last = e.transpose(wf_ps[:, tt * 128:(tt + 1) * 128],
                                           wtok_sb[:, tt * 8:(tt + 1) * 8], idf_sb[:])
                    return last
                pe_wf = P.emit("tensor", wfs, waits=[("DV", dv_t2)], inc="PE")
                dv_wf = P.emit("vector", lambda e: e.tensor_copy(wf_sb[:], wf_ps[:]),
                               waits=[("PE", pe_wf)], inc="DV")
                def wexp(e):
                    last = None
                    for hf in (0, 1):
                        last = e.matmul(wexp_ps[:, hf * 512:(hf + 1) * 512], e2_sb[:],
                                        wf_sb[:, hf * 512:(hf + 1) * 512], start=True, stop=True)
                    return last
                pe_we = P.emit("tensor", wexp, waits=[("DV", dv_wf)], inc="PE")
                dv_hw = P.emit("vector",
                               lambda e: e.tensor_tensor(hw_sb[:], h_sb[:], wexp_ps[:], ALU.mult),
                               waits=[("PE", pe_we), ("DV", dv_h)], inc="DV")

            with ExitStack() as c2b:
                e2b = c2b.enter_context
                y_ps = [e2b(nc.psum_tensor(f"y_ps{i}", [128, TOWN], F32)) for i in range(2)]
                tr_ps = [e2b(nc.psum_tensor(f"tr_ps{i}", [128, TOWN], F32)) for i in range(2)]

                y_free = [(None, 0), (None, 0)]
                x2c_free = [(None, 0), (None, 0)]
                tr_free = [(None, 0), (None, 0)]
                dv_tre = 0
                for oc in range(32):
                    wb = load_slab2(wbase[oc * 128:(oc + 1) * 128, :])
                    pe_y = 0
                    for hf in (0, 1):
                        for sub in range(4):
                            mms = [(wsl2[wb][:, kc * 128:(kc + 1) * 128],
                                    x1_sb[:, kc * TOWN + hf * 512: kc * TOWN + hf * 512 + 512])
                                   for kc in range(sub * 8, sub * 8 + 8)]
                            if sub == 3:
                                mms.append((wlb_sb[:, oc * 128:(oc + 1) * 128],
                                            hw_sb[:, hf * 512:(hf + 1) * 512]))
                            waits = [("DV", dv_ap_at[sub * 8 + 7])]
                            if hf == 0 and sub == 0:
                                waits += [("WA" if wb == 0 else "WB", slab2_ready[wb]),
                                          y_free[oc % 2]]
                            if sub == 3:
                                waits.append(("DV", dv_hw))
                            def fny(e, ms=list(mms), oc=oc, hf=hf, sub=sub):
                                last = None
                                for kx, (l, r) in enumerate(ms):
                                    last = e.matmul(y_ps[oc % 2][:, hf * 512:(hf + 1) * 512],
                                                    l, r, start=(sub == 0 and kx == 0),
                                                    stop=(sub == 3 and kx == len(ms) - 1),
                                                    skip_group_check=True)
                                return last
                            pe_y = P.emit("tensor", fny, waits, inc="PE")
                    slab2_lastuse[wb] = pe_y
                    dv_x2 = P.emit("vector",
                                   lambda e, oc=oc: e.scalar_tensor_tensor(
                                       x2c_sb[oc % 2][:], y_ps[oc % 2][:],
                                       bb_sb[:, oc:oc + 1],
                                       x1_sb[:, oc * TOWN:(oc + 1) * TOWN],
                                       ALU.add, ALU.add),
                                   waits=[("PE", pe_y), x2c_free[oc % 2]], inc="DV")
                    y_free[oc % 2] = ("DV", dv_x2)
                    def trs(e, oc=oc):
                        last = None
                        for tt in range(8):
                            last = e.transpose(tr_ps[oc % 2][:, tt * 128:(tt + 1) * 128],
                                               x2c_sb[oc % 2][:, tt * 128:(tt + 1) * 128],
                                               idf_sb[:])
                        return last
                    pe_tr = P.emit("tensor", trs,
                                   waits=[("DV", dv_x2), tr_free[oc % 2]], inc="PE")
                    dv_tre = P.emit("vector",
                                    lambda e, oc=oc: e.tensor_copy(
                                        x2tok_sb[:].rearrange("p (tt f) -> p tt f", tt=8)[:, :, oc * 128:(oc + 1) * 128],
                                        tr_ps[oc % 2][:].rearrange("p (tt f) -> p tt f", tt=8)),
                                    waits=[("PE", pe_tr)], inc="DV")
                    ac_sq2 = P.emit("scalar",
                                    lambda e, oc=oc: e.activation(
                                        x2c_sb[oc % 2][:], tr_ps[oc % 2][:], AF.Square),
                                    waits=[("PE", pe_tr), ("DV", dv_tre)], inc="AC")
                    def parts(e, oc=oc):
                        last = None
                        for tt in range(8):
                            e.reduce_sum(s1p_sb[:, tt * 32 + oc: tt * 32 + oc + 1],
                                         tr_ps[oc % 2][:, tt * 128:(tt + 1) * 128], axis=AX)
                            last = e.reduce_sum(s2p_sb[:, tt * 32 + oc: tt * 32 + oc + 1],
                                                x2c_sb[oc % 2][:, tt * 128:(tt + 1) * 128], axis=AX)
                        return last
                    dv_part = P.emit("vector", parts,
                                     waits=[("AC", ac_sq2)], inc="DV", nops=16)
                    x2c_free[oc % 2] = ("DV", dv_part)
                    tr_free[oc % 2] = [("DV", dv_part), ("AC", ac_sq2), ("DV", dv_tre)]

                sy_out = 0
                for tt in range(8):
                    def stats2(e, tt=tt):
                        e.reduce_sum(s1_sb[:, tt:tt + 1], s1p_sb[:, tt * 32:(tt + 1) * 32], axis=AX)
                        e.reduce_sum(s2_sb[:, tt:tt + 1], s2p_sb[:, tt * 32:(tt + 1) * 32], axis=AX)
                        e.tensor_scalar(mean2_sb[:, tt:tt + 1], s1_sb[:, tt:tt + 1], 1.0 / D, None, ALU.mult)
                        e.tensor_scalar(var2_sb[:, tt:tt + 1], s2_sb[:, tt:tt + 1], 1.0 / D, None, ALU.mult)
                        e.tensor_tensor(rs2_sb[:, tt:tt + 1], mean2_sb[:, tt:tt + 1],
                                        mean2_sb[:, tt:tt + 1], ALU.mult)
                        return e.tensor_tensor(var2_sb[:, tt:tt + 1], var2_sb[:, tt:tt + 1],
                                               rs2_sb[:, tt:tt + 1], ALU.subtract)
                    dv_s2 = P.emit("vector", stats2, waits=[("DV", dv_part)], inc="DV", nops=6)
                    ac_sd = P.emit("scalar",
                                   lambda e, tt=tt: e.activation(
                                       nm2_sb[:, tt:tt + 1], var2_sb[:, tt:tt + 1], AF.Sqrt, bias=eps_sb[:]),
                                   waits=[("DV", dv_s2)], inc="AC")
                    def fin3(e, tt=tt):
                        e.reciprocal(rs2_sb[:, tt:tt + 1], nm2_sb[:, tt:tt + 1])
                        e.tensor_tensor(nm2_sb[:, tt:tt + 1], mean2_sb[:, tt:tt + 1],
                                        rs2_sb[:, tt:tt + 1], ALU.mult)
                        return e.tensor_scalar(nm2_sb[:, tt:tt + 1], nm2_sb[:, tt:tt + 1],
                                               -1.0, None, ALU.mult)
                    dv_f3 = P.emit("vector", fin3, waits=[("AC", ac_sd)], inc="DV", nops=3)
                    dv_af = P.emit("vector",
                                   lambda e, tt=tt: e.tensor_scalar(
                                       ofin_sb[:], x2tok_sb[:, tt * D:(tt + 1) * D],
                                       rs2_sb[:, tt:tt + 1], nm2_sb[:, tt:tt + 1],
                                       ALU.mult, ALU.add),
                                   waits=[("DV", dv_f3), ("OT", sy_out)], inc="DV")
                    sy_out = P.dma("sync", "OT", out[tt * 128:(tt + 1) * 128, :], ofin_sb[:],
                                   waits=[("DV", dv_af)])

        P.emit("sync", lambda e: e.nop(), waits=[("OT", C["OT"])])

        class _ChainEng:
            def __init__(self, eng, sem):
                self._eng = eng
                self._sem = sem
                self.count = 0

            def __getattr__(self, name):
                f = getattr(self._eng, name)
                if name in ("wait_ge", "partition_id"):
                    return f

                def call(*a, **kw):
                    if self.count:
                        self._eng.wait_ge(self._sem, self.count)
                    inst = f(*a, **kw)
                    inst.then_inc(self._sem, 1)
                    self.count += 1
                    return inst
                return call

        with nc.Block() as block:
            def mkrep(name):
                def rep(eng):
                    maxw = {}
                    chain = name in ("vector", "scalar")
                    ceng = _ChainEng(eng, sems["DV" if name == "vector" else "AC"]) if chain else None
                    for waits, inc, n, fn in P.ops[name]:
                        for s, v in waits:
                            if v and v > maxw.get(s, 0):
                                eng.wait_ge(sems[s], v)
                                maxw[s] = v
                        if chain and isinstance(inc, tuple):
                            _, sem, pre, nops = inc
                            if pre and pre > maxw.get(sem, 0):
                                eng.wait_ge(sems[sem], pre)
                                maxw[sem] = pre
                            assert ceng.count == pre, (name, ceng.count, pre)
                            fn(ceng)
                            assert ceng.count == pre + nops, (name, ceng.count, pre, nops)
                        else:
                            inst = fn(eng)
                            if inc is not None and inst is not None:
                                inst.then_inc(sems[inc], n)
                return rep

            block.sync(mkrep("sync"))
            block.gpsimd(mkrep("gpsimd"))
            block.tensor(mkrep("tensor"))
            block.vector(mkrep("vector"))
            block.scalar(mkrep("scalar"))

    return nc


def _prep_lhsT(w_rows):
    f = w_rows.shape[0]
    t = w_rows.reshape(f // 128, 128, 32, 128).transpose(0, 3, 2, 1)
    return np.ascontiguousarray(t.reshape(f, 4096), dtype=np.float32)


def _swap_rows(w, half):
    idx = np.empty(512, dtype=np.int64)
    n = 0
    for sc in range(4):
        for m in range(128):
            h_loc = 2 * sc + m // 64
            r = m % 64
            idx[n] = half * 2048 + h_loc * 256 + (r ^ 1)
            n += 1
    return w[idx]


def kernel(L0_attn_hidden_states, L0_attn_w_q, L0_attn_w_k, L0_attn_w_v,
           L0_attn_w_out, L0_ffn_W_base, L0_ffn_b_base, L0_ffn_W_router,
           L0_ffn_lora_A, L0_ffn_lora_B):
    hs = np.asarray(L0_attn_hidden_states, dtype=np.float32)
    w_q = np.asarray(L0_attn_w_q, dtype=np.float32)
    w_k = np.asarray(L0_attn_w_k, dtype=np.float32)
    w_v = np.asarray(L0_attn_w_v, dtype=np.float32)
    w_out = np.asarray(L0_attn_w_out, dtype=np.float32)
    W_base = np.asarray(L0_ffn_W_base, dtype=np.float32)
    b_base = np.asarray(L0_ffn_b_base, dtype=np.float32)
    W_router = np.asarray(L0_ffn_W_router, dtype=np.float32)
    lora_A = np.asarray(L0_ffn_lora_A, dtype=np.float32)
    lora_B = np.asarray(L0_ffn_lora_B, dtype=np.float32)

    hst_b = [np.ascontiguousarray(hs[b].T) for b in range(4)]
    perm = np.arange(2048)
    for h in range(8):
        if h % 2 == 1:
            c0 = h * 256
            perm[c0:c0 + 128] = np.concatenate(
                [np.arange(c0 + 64, c0 + 128), np.arange(c0, c0 + 64)])
    wq_h = [_prep_lhsT(w_q[h * 2048:(h + 1) * 2048][perm]) for h in (0, 1)]
    wk_h = [_prep_lhsT(w_k[h * 2048:(h + 1) * 2048][perm]) for h in (0, 1)]
    wv_h = []
    for h in (0, 1):
        t = w_v[h * 2048:(h + 1) * 2048].reshape(8, 256, 32, 128).transpose(0, 3, 2, 1)
        wv_h.append(np.ascontiguousarray(t.reshape(1024, 8192), dtype=np.float32))
    wout_h = []
    for h in (0, 1):
        perm_io = np.concatenate([np.arange(h * 2048, (h + 1) * 2048),
                                  np.arange((1 - h) * 2048, (2 - h) * 2048)])
        wout_h.append(_prep_lhsT(np.ascontiguousarray(w_out[:, perm_io])))
    wbase_p = _prep_lhsT(W_base)
    bb_p = np.ascontiguousarray(b_base.reshape(32, 128).T)
    wr_p = np.ascontiguousarray(
        W_router.T.reshape(32, 128, 8).transpose(1, 0, 2).reshape(128, 256))
    a_cat = lora_A.reshape(128, 4096)
    wla_p = np.ascontiguousarray(
        a_cat.T.reshape(32, 128, 128).transpose(1, 0, 2).reshape(128, 4096))
    wlb_p = np.ascontiguousarray(lora_B.transpose(0, 2, 1).reshape(128, 4096))

    inv = (1.0 / np.float32(10000.0) ** (np.arange(0, 64, 2, dtype=np.float32)
                                         / np.float32(64.0))).astype(np.float32)
    ang = (np.arange(S, dtype=np.float32)[None, :] * inv[:, None]).astype(np.float32)
    cosj = np.cos(ang).astype(np.float32)
    sinj = np.sin(ang).astype(np.float32)
    cos2 = np.empty((128, S), np.float32)
    sin2 = np.empty((128, S), np.float32)
    cos2[0:64:2] = cosj; cos2[1:64:2] = cosj
    sin2[0:64:2] = -sinj; sin2[1:64:2] = sinj
    cos2[64:128] = cos2[0:64]
    sin2[64:128] = sin2[0:64]
    identb = np.eye(128, dtype=np.float32).astype(bf16)
    identf = np.eye(128, dtype=np.float32)
    q_i = np.arange(128)[:, None]
    k_i = np.arange(128)[None, :]
    maskt = np.where(k_i <= q_i, 0.0, NEG).astype(np.float32)
    masktb = maskt.astype(bf16)
    onesb = np.ones((128, 1), np.float32).astype(bf16)
    onesf = np.ones((1, 128), np.float32)
    e2f = np.zeros((8, 128), np.float32)
    epsv_h = np.full((128, 1), 1.0e-5, dtype=np.float32)
    for e in range(8):
        e2f[e, e * 16:(e + 1) * 16] = 1.0

    nc = build_nc()
    in_maps = []
    for c in range(8):
        b, half = c // 2, c % 2
        in_maps.append({
            "hst": hst_b[b],
            "hsto": np.ascontiguousarray(hst_b[b][:, half * TOWN:(half + 1) * TOWN]),
            "wq": wq_h[half], "wk": wk_h[half],
            "wv": wv_h[half],
            "wout": wout_h[half], "wbase": wbase_p, "bb": bb_p,
            "wr": wr_p, "wla": wla_p, "wlb": wlb_p,
            "e2f": e2f, "cosd": cos2, "sind": sin2,
            "identb": identb, "identf": identf, "masktb": masktb,
            "onesb": onesb, "onesf": onesf, "epsv": epsv_h,
        })
    res = run_bass_kernel_spmd(nc, in_maps, core_ids=list(range(8)))
    outp = np.empty((4, S, D), dtype=np.float32)
    for c in range(8):
        b, half = c // 2, c % 2
        outp[b, half * TOWN:(half + 1) * TOWN, :] = res.results[c]["out"]
    return outp



# version 21 (final)
# speedup vs baseline: 1.0503x; 1.0503x over previous
import numpy as np
import ml_dtypes
from contextlib import ExitStack

import concourse.bass as bass
import concourse.mybir as mybir
from concourse.bass_utils import run_bass_kernel_spmd

F32 = mybir.dt.float32
BF16 = mybir.dt.bfloat16
AF = mybir.ActivationFunctionType
ALU = mybir.AluOpType
AX = mybir.AxisListType.X
bf16 = ml_dtypes.bfloat16

S = 2048
D = 4096
CF = 2048
NKC = 32
NFC = 16
TOWN = 1024
NEG = -1.0e9
RG = [[0, 1], [2, 3], [4, 5], [6, 7]]


class _Plan:
    STREAMS = ("CS", "HS", "HS2", "HS3", "HS4", "WA", "WB", "WV", "WV2", "GS", "SPA", "SPB", "VS", "VS2", "KL", "KL2", "KL3", "KL4",
               "QL", "OS", "OL", "OL2", "HA", "HB", "OT", "PE", "DV", "AC", "CC")

    def __init__(self):
        self.ops = {"sync": [], "gpsimd": [], "tensor": [], "vector": [], "scalar": []}
        self.cnt = {k: 0 for k in self.STREAMS}

    def emit(self, eng, fn, waits=(), inc=None, n=1, nops=1):
        w = []
        for item in waits:
            if not item:
                continue
            if isinstance(item, list):
                w += [x for x in item if x and x[1]]
            elif item[1]:
                w.append(item)
        if eng in ("vector", "scalar") and inc in (None, "DV" if eng == "vector" else "AC"):
            sem = "DV" if eng == "vector" else "AC"
            pre = self.cnt[sem]
            self.ops[eng].append((w, ("CHAIN", sem, pre, nops), nops, fn))
            self.cnt[sem] += nops
            return self.cnt[sem]
        self.ops[eng].append((w, inc, n, fn))
        if inc is not None:
            self.cnt[inc] += n
            return self.cnt[inc]
        return None

    def dma(self, eng, sem, dst, src, waits=()):
        return self.emit(eng, lambda e, d=dst, s=src: e.dma_start(out=d, in_=s),
                         waits, inc=sem, n=16)


def build_nc():
    nc = bass.Bass("TRN2", target_bir_lowering=False)
    di = lambda name, shape, dt: nc.dram_tensor(name, shape, dt, kind="ExternalInput").ap()

    hst = di("hst", [D, S], F32)
    hsto = di("hsto", [D, TOWN], F32)
    wq = di("wq", [NFC * 128, D], F32)
    wk = di("wk", [NFC * 128, D], F32)
    wv = di("wv", [8 * 128, 8192], F32)
    wout = di("wout", [32 * 128, D], F32)
    wbase = di("wbase", [32 * 128, D], F32)
    bb = di("bb", [128, 32], F32)
    wr = di("wr", [128, 256], F32)
    wla = di("wla", [128, D], F32)
    wlb = di("wlb", [128, D], F32)
    e2f = di("e2f", [8, 128], F32)
    cosd = di("cosd", [128, S], F32)
    sind = di("sind", [128, S], F32)
    identb = di("identb", [128, 128], BF16)
    identf = di("identf", [128, 128], F32)
    masktb = di("masktb", [128, 128], BF16)
    onesb = di("onesb", [128, 1], BF16)
    onesf = di("onesf", [1, 128], F32)
    epsv = di("epsv", [128, 1], F32)

    out = nc.dram_tensor("out", [TOWN, D], F32, kind="ExternalOutput").ap()

    q_s = nc.dram_tensor("q_s", [CF, S], BF16).ap()
    k_s = nc.dram_tensor("k_s", [CF, S], BF16).ap()
    v_s = nc.dram_tensor("v_s", [8 * 16 * 128, 256], BF16).ap()
    o_s = nc.dram_tensor("o_s", [4 * CF, 512], BF16)
    o_g = nc.dram_tensor("o_g", [4 * 2 * CF, 512], BF16)

    P = _Plan()
    C = P.cnt
    dyn_state = {}

    def dyn_col(base_ap, width):
        def mk(e, b=base_ap):
            if "toff" not in dyn_state:
                pid = e.partition_id()
                dyn_state["toff"] = (pid % 2) * TOWN
            return bass.AP(b.tensor, b.offset + dyn_state["toff"], b.ap)
        return mk

    with ExitStack() as top:
        et = top.enter_context
        sems = {k: et(nc.semaphore(k)) for k in _Plan.STREAMS}

        id_sb = et(nc.sbuf_tensor("id_sb", [128, 128], BF16))
        idf_sb = et(nc.sbuf_tensor("idf_sb", [128, 128], F32))
        maskb_sb = et(nc.sbuf_tensor("maskb_sb", [128, 128], BF16))
        ones_sb = et(nc.sbuf_tensor("ones_sb", [128, 1], BF16))
        onesf_sb = et(nc.sbuf_tensor("onesf_sb", [1, 128], F32))
        e2_sb = et(nc.sbuf_tensor("e2_sb", [8, 128], F32))
        bb_sb = et(nc.sbuf_tensor("bb_sb", [128, 32], F32))
        eps_sb = et(nc.sbuf_tensor("eps_sb", [128, 1], F32))

        for dst, src in ((id_sb[:], identb), (idf_sb[:], identf),
                         (maskb_sb[:], masktb),
                         (ones_sb[:], onesb), (onesf_sb[:], onesf), (e2_sb[:], e2f),
                         (bb_sb[:], bb), (eps_sb[:], epsv)):
            P.dma("sync", "CS", dst, src)

        with ExitStack() as pa:
            ea = pa.enter_context
            hs_sb = ea(nc.sbuf_tensor("hs_sb", [128, NKC * S], BF16))
            pa1 = ExitStack()
            e1a = pa1.enter_context
            cos_sb = e1a(nc.sbuf_tensor("cos_sb", [128, S], F32))
            sin_sb = e1a(nc.sbuf_tensor("sin_sb", [128, S], F32))
            wsl = [e1a(nc.sbuf_tensor(f"wsl{i}", [128, D], BF16)) for i in range(2)]
            sw_sb = e1a(nc.sbuf_tensor("sw_sb", [128, 512], F32))
            stage = [e1a(nc.sbuf_tensor(f"stage{i}", [128, S], BF16)) for i in range(2)]
            r1_sb = e1a(nc.sbuf_tensor("r1_sb", [128, 512], F32))
            r2_sb = e1a(nc.sbuf_tensor("r2_sb", [128, 512], F32))
            pk = [e1a(nc.psum_tensor(f"pk{i}", [128, 512], F32)) for i in range(2)]

            P.dma("sync", "CS", cos_sb[:], cosd)
            P.dma("sync", "CS", sin_sb[:], sind)
            cs_tot = C["CS"]

            hs3 = hst.rearrange("(kc p) t -> p kc t", p=128)
            hsb3 = hs_sb[:].rearrange("p (kc t) -> p kc t", kc=NKC)
            hs_streams = ("HS", "HS2", "HS3", "HS4")
            gp_hs_part = []
            for g in range(4):
                v = P.dma("gpsimd", hs_streams[g], hsb3[:, g * 8:(g + 1) * 8, :],
                          hs3[:, g * 8:(g + 1) * 8, :])
                gp_hs_part.append((hs_streams[g], v))

            slab_i = 0
            slab_lastuse = [0, 0]
            slab_ready = [0, 0]
            pg = 0
            pk_free = [[], []]
            stage_st = [0, 0]

            def load_slab(src_rows, dst=None):
                nonlocal slab_i
                buf = slab_i % 2
                tgt = wsl[buf][:] if dst is None else dst
                v = P.dma("gpsimd", "WA" if buf == 0 else "WB", tgt, src_rows,
                          waits=[("PE", slab_lastuse[buf])])
                slab_ready[buf] = v
                slab_i += 1
                return buf

            def mm_group(psum_ap, mms, waits, inc="PE"):
                def fn(e, ms=list(mms), pa_=psum_ap):
                    last = None
                    for k, (l, r) in enumerate(ms):
                        last = e.matmul(pa_, l, r, start=(k == 0), stop=(k == len(ms) - 1))
                    return last
                return P.emit("tensor", fn, waits, inc=inc)

            def proj_group(wbuf, ts, psum_buf, extra_waits=()):
                last = 0
                for g4 in range(4):
                    waits = [gp_hs_part[g4]]
                    if g4 == 0:
                        waits += [("WA" if wbuf == 0 else "WB", slab_ready[wbuf]),
                                  pk_free[psum_buf]] + list(extra_waits)
                    def fng(e, g4=g4, wbuf=wbuf, ts=ts, psum_buf=psum_buf):
                        lastmm = None
                        for kc in range(g4 * 8, g4 * 8 + 8):
                            lastmm = e.matmul(pk[psum_buf][:],
                                              wsl[wbuf][:, kc * 128:(kc + 1) * 128],
                                              hs_sb[:, kc * S + ts * 512: kc * S + ts * 512 + 512],
                                              start=(kc == g4 * 8 and g4 == 0),
                                              stop=(kc == 31 and g4 == 3),
                                              skip_group_check=True)
                        return lastmm
                    last = P.emit("tensor", fng, waits, inc="PE")
                return last

            PSWAP = [i ^ 1 for i in range(32)]

            def do_qk(w_main, dst_dram):
                nonlocal pg
                for fc in range(NFC):
                    h = fc // 2
                    mbuf = load_slab(w_main[fc * 128:(fc + 1) * 128, :])
                    sbuf_stage = stage[fc % 2]
                    pe_last = 0
                    ev_dv = ev_ac = 0
                    for ts in range(4):
                        pb = pg % 2
                        extra = ([("SPA" if fc % 2 == 0 else "SPB", stage_st[fc % 2]), ("CS", cs_tot)]
                                 if ts == 0 else [])
                        pe_last = proj_group(mbuf, ts, pb, extra)
                        if fc % 2 == 0:
                            hb = (h % 2) * 64
                            nb = 64 - hb
                            def rope(e, pb=pb, ts=ts, hb=hb, sg=sbuf_stage):
                                e.stream_shuffle(sw_sb[hb:hb + 64, :], pk[pb][hb:hb + 64, :], PSWAP)
                                e.tensor_tensor(r1_sb[hb:hb + 64, :], pk[pb][hb:hb + 64, :],
                                                cos_sb[hb:hb + 64, ts * 512:(ts + 1) * 512], ALU.mult)
                                e.tensor_tensor(r2_sb[hb:hb + 64, :], sw_sb[hb:hb + 64, :],
                                                sin_sb[hb:hb + 64, ts * 512:(ts + 1) * 512], ALU.mult)
                                return e.tensor_tensor(sg[hb:hb + 64, ts * 512:(ts + 1) * 512],
                                                       r1_sb[hb:hb + 64, :], r2_sb[hb:hb + 64, :], ALU.add)
                            ev_dv = P.emit("vector", rope, waits=[("PE", pe_last)], inc="DV", nops=4)
                            ev_ac = P.emit("scalar",
                                           lambda e, pb=pb, ts=ts, nb=nb, sg=sbuf_stage: e.copy(
                                               sg[nb:nb + 64, ts * 512:(ts + 1) * 512], pk[pb][nb:nb + 64, :]),
                                           waits=[("PE", pe_last)], inc="AC")
                            pk_free[pb] = [("AC", ev_ac), ("DV", ev_dv)]
                        else:
                            ev_ac = P.emit("scalar",
                                           lambda e, pb=pb, ts=ts, sg=sbuf_stage: e.copy(
                                               sg[:, ts * 512:(ts + 1) * 512], pk[pb][:]),
                                           waits=[("PE", pe_last)], inc="AC")
                            pk_free[pb] = [("AC", ev_ac)]
                        pg += 1
                    slab_lastuse[mbuf] = pe_last
                    stage_st[fc % 2] = P.dma(
                        "sync", "SPA" if fc % 2 == 0 else "SPB",
                        dst_dram[fc * 128:(fc + 1) * 128, :], sbuf_stage[:],
                        waits=[("DV", ev_dv), ("AC", ev_ac)])

            do_qk(wk, k_s)
            do_qk(wq, q_s)
            pa1.close()
            a1f = dict(C)
            a1bar = [("PE", a1f["PE"]), ("DV", a1f["DV"]), ("AC", a1f["AC"]),
                     ("SPA", a1f["SPA"]), ("SPB", a1f["SPB"])]

            wvsl = [ea(nc.sbuf_tensor(f"wvsl{i}", [128, 8192], BF16)) for i in range(2)]
            vst_sb = [ea(nc.sbuf_tensor(f"vst_sb{i}", [128, 16 * 256], BF16)) for i in range(2)]
            pv = [ea(nc.psum_tensor(f"pv{i}", [128, 256], F32)) for i in range(2)]
            v4 = v_s.rearrange("(dc tt p) n -> p dc tt n", p=128, tt=16)
            wv_lastuse = [0, 0]
            wv_ready = [0, 0]
            vst_st = [0, 0]
            pvg = 0
            pv_free = [(None, 0), (None, 0)]
            for dc in range(8):
                vb = dc % 2
                wv_ready[vb] = P.dma("gpsimd", "WV" if vb == 0 else "WV2", wvsl[vb][:],
                                     wv[dc * 128:(dc + 1) * 128, :],
                                     waits=([("PE", wv_lastuse[vb])] if dc >= 2 else a1bar))
                ev_last = 0
                for tt in range(16):
                    pb = pvg % 2
                    waits = [("WV" if vb == 0 else "WV2", wv_ready[vb]), pv_free[pb]]
                    if tt == 0:
                        waits.append(("VS" if vb == 0 else "VS2", vst_st[vb]))
                    if dc == 0 and tt == 0:
                        waits += [("DV", a1f["DV"]), ("AC", a1f["AC"])]
                    mms = [(hs_sb[:, kc * S + tt * 128: kc * S + tt * 128 + 128],
                            wvsl[vb][:, kc * 256:(kc + 1) * 256]) for kc in range(NKC)]
                    pe_v = mm_group(pv[pb][:], mms, list(gp_hs_part) + waits)
                    ev_last = P.emit("scalar",
                                     lambda e, pb=pb, tt=tt, vb=vb: e.copy(
                                         vst_sb[vb][:, tt * 256:(tt + 1) * 256], pv[pb][:]),
                                     waits=[("PE", pe_v)], inc="AC")
                    pv_free[pb] = ("AC", ev_last)
                    pvg += 1
                wv_lastuse[vb] = pe_v
                vst_st[vb] = P.dma("sync", "VS" if vb == 0 else "VS2", v4[:, dc, :, :],
                                   vst_sb[vb][:].rearrange("p (tt n) -> p tt n", tt=16),
                                   waits=[("AC", ev_last)])

        a_fin = dict(C)

        with ExitStack() as pb_:
            eb = pb_.enter_context
            it = 0
            k_sb = eb(nc.sbuf_tensor("k_sb", [128, NFC * S], BF16))
            v_sb = eb(nc.sbuf_tensor("v_sb", [128, 8 * 16 * 256], BF16))
            q_sb = eb(nc.sbuf_tensor("q_sb", [128, NFC * 512], BF16))
            p_sb = [eb(nc.sbuf_tensor(f"p_sb{i}", [128, S], BF16)) for i in range(4)]
            pt_sb = [eb(nc.sbuf_tensor(f"pt_sb{i}", [128, S], BF16)) for i in range(2)]
            ost_sb = eb(nc.sbuf_tensor("ost_sb", [128, NFC * 512], BF16))
            z_sb = eb(nc.sbuf_tensor("z_sb", [128, 8], F32))
            rz_sb = eb(nc.sbuf_tensor("rz_sb", [128, 8], F32))
            R6 = eb(nc.psum_tensor("R6", [128, 6 * 512], F32))
            av_ps = [eb(nc.psum_tensor(f"av_ps{i}", [128, 256], F32)) for i in range(2)]

            bar = [("PE", a_fin["PE"]), ("DV", a_fin["DV"]), ("AC", a_fin["AC"]),
                   ("SPA", a_fin["SPA"]), ("SPB", a_fin["SPB"]), ("VS", a_fin["VS"]),
                   ("VS2", a_fin["VS2"])]
            k3 = k_s.rearrange("(fc p) t -> p fc t", p=128)
            kb3 = k_sb[:].rearrange("p (fc t) -> p fc t", fc=NFC)
            v4r = v_s.rearrange("(dc tt p) n -> p dc tt n", p=128, tt=16)
            vb4 = v_sb[:].rearrange("p (dc tt n) -> p dc tt n", dc=8, tt=16)
            q3 = q_s.rearrange("(fc p) t -> p fc t", p=128)
            qb3 = q_sb[:].rearrange("p (fc t) -> p fc t", fc=NFC)
            kv_ready = []
            kl_streams = ("KL", "KL2", "KL3", "KL4")
            for g in range(4):
                cs0, cs1 = g * 512, (g + 1) * 512
                sy_kv = P.dma("sync", kl_streams[g], kb3[:, :, cs0:cs1],
                              k3[:, :, cs0:cs1], waits=bar if g == 0 else ())
                for dc in range(8):
                    sy_kv = P.dma("sync", kl_streams[g], vb4[:, dc, g * 4:(g + 1) * 4, :],
                                  v4r[:, dc, g * 4:(g + 1) * 4, :])
                kv_ready.append((kl_streams[g], sy_kv))

            bank_ev = [("DV", 0)] * 6

            def alloc(nb):
                while True:
                    for s0 in range(0, 7 - nb):
                        if all(bank_ev[i] is not None for i in range(s0, s0 + nb)):
                            waits = [bank_ev[i] for i in range(s0, s0 + nb)]
                            for i in range(s0, s0 + nb):
                                bank_ev[i] = None
                            return s0, waits
                    if pends:
                        back(pends.pop(0))
                    else:
                        raise RuntimeError("psum arena full")

            def release(s0, nb, ev):
                if ev is None:
                    return
                for i in range(s0, s0 + nb):
                    bank_ev[i] = ev

            pexp_free = [("PE", 0)] * 4
            zslot_free = [("DV", 0)] * 4
            pt_free = [("PE", 0)] * 2
            av_free = [("DV", 0)] * 2
            q_lastuse = 0
            sy_q = 0
            sy_ost = 0
            o4 = o_s.ap().rearrange("(sb fc p) t -> p sb fc t", p=128, sb=4)
            cc_vals = []
            pends = []

            def sc_part(h, qt, c0, ew, bank0, extra_waits, add_mask):
                pe_sc = 0
                off = c0
                bk = bank0
                while off < c0 + ew:
                    w = min(512, c0 + ew - off)
                    mms = [(q_sb[:, (2 * h + dk) * 512 + qt * 128: (2 * h + dk) * 512 + qt * 128 + 128],
                            k_sb[:, (2 * h + dk) * S + off: (2 * h + dk) * S + off + w])
                           for dk in (0, 1)]
                    last_chunk = (off + w >= c0 + ew)
                    def fn(e, ms=list(mms), bk=bk, w=w, last=last_chunk and add_mask):
                        out = R6[:, bk * 512: bk * 512 + w]
                        lastmm = None
                        n = len(ms)
                        for kx, (l, r) in enumerate(ms):
                            lastmm = e.matmul(out, l, r, start=(kx == 0),
                                              stop=(kx == n - 1 and not last),
                                              skip_group_check=True)
                        if last:
                            lastmm = e.matmul(R6[:, bk * 512 + w - 128: bk * 512 + w],
                                              id_sb[:], maskb_sb[:], start=False, stop=True,
                                              skip_group_check=True)
                        return lastmm
                    pe_sc = P.emit("tensor", fn, extra_waits if off == c0 else (), inc="PE")
                    off += w
                    bk += 1
                return pe_sc

            def front(it, sb, qt, h):
                i_glob = sb * 4 + qt
                ext = (i_glob + 1) * 128
                nsub = i_glob + 1
                pr = it % 4
                zs = it % 4
                extA = min(ext, 1024)
                extB = ext - extA
                nbA = max(2, (extA + 511) // 512)
                a0, wA = alloc(nbA)
                pe_scA = sc_part(h, qt, 0, extA, a0,
                                 wA + [("QL", sy_q), ("CS", cs_tot),
                                       kv_ready[(extA - 1) // 512]], extB == 0)
                ac_expA = P.emit("scalar",
                                 lambda e, a0=a0, extA=extA, pr=pr, zs=zs: e.activation(
                                     p_sb[pr][:, 0:extA], R6[:, a0 * 512: a0 * 512 + extA],
                                     AF.Exp, scale=0.0625,
                                     accum_out=z_sb[:, zs * 2: zs * 2 + 1]),
                                 waits=[("PE", pe_scA), pexp_free[pr], zslot_free[zs]],
                                 inc="AC")
                if extB:
                    release(a0, nbA, ("AC", ac_expA))
                    nbB = max(2, (extB + 511) // 512)
                    b0, wB = alloc(nbB)
                    pe_scB = sc_part(h, qt, 1024, extB, b0,
                                     wB + [kv_ready[(ext - 1) // 512]], True)
                    P.emit("scalar",
                           lambda e, b0=b0, extB=extB, pr=pr, zs=zs: e.activation(
                               p_sb[pr][:, 1024:1024 + extB],
                               R6[:, b0 * 512: b0 * 512 + extB],
                               AF.Exp, scale=0.0625,
                               accum_out=z_sb[:, zs * 2 + 1: zs * 2 + 2]),
                           waits=[("PE", pe_scB)], inc="AC")
                    P.emit("scalar",
                           lambda e, zs=zs: e.add(z_sb[:, zs * 2: zs * 2 + 1],
                                                  z_sb[:, zs * 2: zs * 2 + 1],
                                                  z_sb[:, zs * 2 + 1: zs * 2 + 2]),
                           inc="AC")
                P.emit("scalar",
                       lambda e, zs=zs: e.activation(rz_sb[:, zs:zs + 1],
                                                     z_sb[:, zs * 2: zs * 2 + 1], AF.Ln),
                       inc="AC")
                P.emit("scalar",
                       lambda e, zs=zs: e.activation(rz_sb[:, 4 + zs:5 + zs],
                                                     rz_sb[:, zs:zs + 1], AF.Exp, scale=-1.0),
                       inc="AC")
                ac_n = P.emit("scalar",
                              lambda e, ext=ext, pr=pr, zs=zs: e.activation(
                                  p_sb[pr][:, 0:ext], p_sb[pr][:, 0:ext], AF.Identity,
                                  scale=rz_sb[:, 4 + zs:5 + zs]),
                              inc="AC")
                if extB:
                    release(b0, nbB, None)
                    win0, win_nb = b0, nbB
                else:
                    release(a0, nbA, None)
                    win0, win_nb = a0, nbA
                zslot_free[zs] = ("AC", ac_n)
                return dict(it=it, sb=sb, qt=qt, h=h, nsub=nsub, pr=pr,
                            win0=win0, win_nb=win_nb, dv_dg=ac_n)

            def back(st):
                nonlocal sy_ost
                bit = st["it"]; sb = st["sb"]; qt = st["qt"]; h = st["h"]
                nsub = st["nsub"]; pr = st["pr"]
                win0 = st["win0"]; win_nb = st["win_nb"]; dv_dg = st["dv_dg"]
                ngr = (nsub + 3) // 4
                pb = bit % 2
                win_ev = [None] * win_nb
                ac_pt = 0
                pe_tr_last = 0
                pe_av = 0

                def avp(g, final):
                    nonlocal pe_av
                    s0 = g * 4
                    s1 = min(nsub, s0 + 4)
                    def fn(e, s0=s0, s1=s1, g=g, h=h, pb=pb, final=final):
                        last = None
                        for dv in (0, 1):
                            for sx in range(s0, s1):
                                last = e.matmul(av_ps[pb][:, dv * 128: dv * 128 + 128],
                                                v_sb[:, h * 4096 + sx * 256 + dv * 128: h * 4096 + sx * 256 + dv * 128 + 128],
                                                pt_sb[pb][:, sx * 128:(sx + 1) * 128],
                                                start=(g == 0 and dv == 0 and sx == s0),
                                                stop=(final and dv == 1 and sx == s1 - 1),
                                                skip_group_check=True)
                        return last
                    w = [win_ev[g % win_nb]]
                    if g == 0:
                        w += [av_free[pb], kv_ready[(nsub - 1) // 4]]
                    pe_av = P.emit("tensor", fn, w, inc="PE")

                for g in range(ngr):
                    wb = win0 + (g % win_nb)
                    s0 = g * 4
                    s1 = min(nsub, s0 + 4)
                    def trg(e, s0=s0, s1=s1, wb=wb, pr=pr, bit=bit):
                        last = None
                        for sx in range(s0, s1):
                            last = e.matmul(R6[:, wb * 512 + (sx - s0) * 128: wb * 512 + (sx - s0) * 128 + 128],
                                            p_sb[pr][:, sx * 128:(sx + 1) * 128],
                                            id_sb[:], start=True, stop=True)
                        return last
                    waits = [("AC", dv_dg)]
                    if win_ev[g % win_nb] is not None:
                        waits.append(win_ev[g % win_nb])
                    pe_tr_last = P.emit("tensor", trg, waits, inc="PE")
                    nw = (s1 - s0) * 128
                    ac_pt = P.emit("vector",
                                   lambda e, s0=s0, nw=nw, wb=wb, pb=pb: e.tensor_copy(
                                       pt_sb[pb][:, s0 * 128: s0 * 128 + nw],
                                       R6[:, wb * 512: wb * 512 + nw]),
                                   waits=[("PE", pe_tr_last)], inc="DV")
                    win_ev[g % win_nb] = ("DV", ac_pt)
                    if g >= 1:
                        avp(g - 1, final=False)
                avp(ngr - 1, final=True)
                pexp_free[pr] = ("PE", pe_tr_last)
                for wq_ in range(win_nb):
                    if win_ev[wq_] is not None:
                        release(win0 + wq_, 1, win_ev[wq_])
                    else:
                        release(win0 + wq_, 1, ("DV", ac_pt))
                pt_free[pb] = ("PE", pe_av)

                owaits = [("PE", pe_av)]
                if bit % 32 == 0:
                    owaits.append(("OS", sy_ost))
                dv_oevac = P.emit(
                    "vector",
                    lambda e, h=h, qt=qt, pb=pb: e.tensor_copy(
                        ost_sb[:].rearrange("p (fc t) -> p fc t", fc=NFC)[:, 2 * h:2 * h + 2, qt * 128:(qt + 1) * 128],
                        av_ps[pb][:].rearrange("p (dv t) -> p dv t", dv=2)),
                    waits=owaits, inc="DV")
                av_free[pb] = ("DV", dv_oevac)
                if bit % 32 == 31:
                    sy_ost = P.dma("sync", "OS",
                                   o4[:, sb, :, :],
                                   ost_sb[:].rearrange("p (fc t) -> p fc t", fc=NFC),
                                   waits=[("DV", dv_oevac)])
                    cc_vals.append(P.emit(
                        "gpsimd",
                        lambda e, sb=sb: e.collective_compute(
                            "AllGather", ALU.bypass, replica_groups=RG,
                            ins=[o_s.ap()[sb * CF:(sb + 1) * CF, :].opt()],
                            outs=[o_g.ap()[sb * 2 * CF:(sb + 1) * 2 * CF, :].opt()]),
                        waits=[("OS", sy_ost)], inc="CC"))

            for sb in range(4):
                sy_q = P.dma("sync", "QL", qb3, q3[:, :, sb * 512:(sb + 1) * 512],
                             waits=[("PE", q_lastuse)] + (bar if sb == 0 else []))
                for qt in range(4):
                    for h in range(8):
                        pends.append(front(it, sb, qt, h))
                        if qt == 3 and h == 7:
                            q_lastuse = C["PE"]
                        if len(pends) > 3:
                            back(pends.pop(0))
                        it += 1
            while pends:
                back(pends.pop(0))

        b_fin = dict(C)

        cc_v = C["CC"]

        with ExitStack() as pc:
            ec = pc.enter_context
            x1_sb = ec(nc.sbuf_tensor("x1_sb", [128, 32 * TOWN], BF16))
            wsl2 = [ec(nc.sbuf_tensor(f"wsl2_{i}", [128, D], BF16)) for i in range(2)]
            slab2_ready = [0, 0]
            slab2_lastuse = [b_fin["PE"], b_fin["PE"]]
            slab2_i = 0

            def load_slab2(src_rows):
                nonlocal slab2_i
                buf = slab2_i % 2
                v = P.dma("gpsimd", "WA" if buf == 0 else "WB", wsl2[buf][:], src_rows,
                          waits=[("PE", slab2_lastuse[buf])])
                slab2_ready[buf] = v
                slab2_i += 1
                return buf

            cbar = [("PE", b_fin["PE"]), ("DV", b_fin["DV"]), ("AC", b_fin["AC"])]

            wla_sb = ec(nc.sbuf_tensor("wla_sb", [128, D], BF16))
            wlb_sb = ec(nc.sbuf_tensor("wlb_sb", [128, D], BF16))
            wr_sb = ec(nc.sbuf_tensor("wr_sb", [128, 256], BF16))
            gp_small = 0
            for dst, src in ((wla_sb[:], wla), (wlb_sb[:], wlb), (wr_sb[:], wr)):
                gp_small = P.dma("gpsimd", "GS", dst, src, waits=[("PE", b_fin["PE"])])
            lg_sb = ec(nc.sbuf_tensor("lg_sb", [8, TOWN], F32))
            h_sb = ec(nc.sbuf_tensor("h_sb", [128, TOWN], BF16))

            with ExitStack() as c1:
                e1 = c1.enter_context
                oo_sb = e1(nc.sbuf_tensor("oo_sb", [128, 32 * TOWN], BF16))
                hso = [e1(nc.sbuf_tensor(f"hso{i}", [128, TOWN], F32)) for i in range(2)]
                lt_sb = e1(nc.sbuf_tensor("lt_sb", [128, TOWN], F32))
                sq_sb = [e1(nc.sbuf_tensor(f"sq_sb{i}", [128, TOWN], BF16)) for i in range(2)]
                m_sb = e1(nc.sbuf_tensor("m_sb", [1, TOWN], F32))
                ex2_sb = e1(nc.sbuf_tensor("ex2_sb", [1, TOWN], F32))
                var_sb = e1(nc.sbuf_tensor("var_sb", [1, TOWN], F32))
                rs_sb = e1(nc.sbuf_tensor("rs_sb", [1, TOWN], F32))
                sd_sb = ex2_sb
                nmrs_sb = var_sb
                po = [e1(nc.psum_tensor(f"po{i}", [128, TOWN], F32)) for i in range(2)]
                st_scope = ExitStack()
                st_ps = st_scope.enter_context(nc.psum_tensor("st_ps", [1, 2048], F32))

                os4 = o_s.ap().rearrange("(sb fc p) t -> p sb fc t", p=128, sb=4)
                og4 = o_g.ap().rearrange("(sb fc p) t -> p sb fc t", p=128, sb=4)
                oo3 = oo_sb[:].rearrange("p (fc t) -> p fc t", t=TOWN)
                sy_own = 0
                for sbl in range(2):
                    base = os4[:, sbl, :, :]
                    def fn_own(e, b=base):
                        if "soff" not in dyn_state:
                            pid = e.partition_id()
                            dyn_state["soff"] = (pid % 2) * (2 * CF * 512)
                            dyn_state["goff"] = (pid % 2) * (2 * 2 * CF * 512) + \
                                ((pid + 1) % 2) * (CF * 512)
                        return bass.AP(b.tensor, b.offset + dyn_state["soff"], b.ap)
                    sy_own = P.emit("sync",
                                    lambda e, fn=fn_own, sbl=sbl: e.dma_start(
                                        out=oo3[:, 0:16, sbl * 512:(sbl + 1) * 512],
                                        in_=fn(e)),
                                    waits=([("OS", b_fin["OS"])] + cbar) if sbl == 0 else (),
                                    inc="OL", n=16)
                sy_oo = 0
                for sbl in range(2):
                    base = og4[:, sbl, 0:16, :]
                    def fn_par(e, b=base):
                        return bass.AP(b.tensor, b.offset + dyn_state["goff"], b.ap)
                    sy_oo = P.emit("sync",
                                   lambda e, fn=fn_par, sbl=sbl: e.dma_start(
                                       out=oo3[:, 16:32, sbl * 512:(sbl + 1) * 512],
                                       in_=fn(e)),
                                   waits=[("CC", cc_v)] if sbl == 0 else (),
                                   inc="OL2", n=16)

                po_free = [(None, 0), (None, 0)]
                sq_free = [(None, 0), (None, 0)]
                hso_free = [(None, 0), (None, 0)]
                dv_x1 = 0
                pe_stat = 0
                stat_pend = None

                def emit_stats(oc, dvv, acv):
                    nonlocal pe_stat
                    def stats(e, oc=oc):
                        last = None
                        for hf in (0, 1):
                            last = e.matmul(st_ps[:, hf * 512:(hf + 1) * 512], ones_sb[:],
                                            x1_sb[:, oc * TOWN + hf * 512: oc * TOWN + hf * 512 + 512],
                                            start=(oc == 0), stop=(oc == 31), skip_group_check=True)
                        for hf in (0, 1):
                            last = e.matmul(st_ps[:, 1024 + hf * 512: 1024 + hf * 512 + 512], ones_sb[:],
                                            sq_sb[oc % 2][:, hf * 512:(hf + 1) * 512],
                                            start=(oc == 0), stop=(oc == 31), skip_group_check=True)
                        return last
                    pe_stat = P.emit("tensor", stats,
                                     waits=[("DV", dvv), ("AC", acv), ("CS", cs_tot)], inc="PE")
                    sq_free[oc % 2] = ("PE", pe_stat)

                for oc in range(32):
                    wb = load_slab2(wout[oc * 128:(oc + 1) * 128, :])
                    sy_h = P.dma("sync", "HA" if oc % 2 == 0 else "HB", hso[oc % 2][:],
                                 hsto[oc * 128:(oc + 1) * 128, :],
                                 waits=[hso_free[oc % 2]])
                    pe_po = 0
                    for hf in (0, 1):
                        mms = [(wsl2[wb][:, kc * 128:(kc + 1) * 128],
                                oo_sb[:, kc * TOWN + hf * 512: kc * TOWN + hf * 512 + 512])
                               for kc in range(16)]
                        waits = ([("WA" if wb == 0 else "WB", slab2_ready[wb]),
                                  ("OL", sy_own), po_free[oc % 2]] if hf == 0 else [])
                        def fno(e, ms=list(mms), oc=oc, hf=hf):
                            last = None
                            for kx, (l, r) in enumerate(ms):
                                last = e.matmul(po[oc % 2][:, hf * 512:(hf + 1) * 512], l, r,
                                                start=(kx == 0), stop=False,
                                                skip_group_check=True)
                            return last
                        P.emit("tensor", fno, waits, inc="PE")
                        mms2 = [(wsl2[wb][:, kc * 128:(kc + 1) * 128],
                                 oo_sb[:, kc * TOWN + hf * 512: kc * TOWN + hf * 512 + 512])
                                for kc in range(16, 32)]
                        def fnp(e, ms=list(mms2), oc=oc, hf=hf):
                            last = None
                            for kx, (l, r) in enumerate(ms):
                                last = e.matmul(po[oc % 2][:, hf * 512:(hf + 1) * 512], l, r,
                                                start=False, stop=(kx == len(ms) - 1),
                                                skip_group_check=True)
                            return last
                        pe_po = P.emit("tensor", fnp, [("OL2", sy_oo)], inc="PE")
                    slab2_lastuse[wb] = pe_po
                    dv_x1 = P.emit("vector",
                                   lambda e, oc=oc: e.tensor_tensor(
                                       x1_sb[:, oc * TOWN:(oc + 1) * TOWN],
                                       po[oc % 2][:], hso[oc % 2][:], ALU.add),
                                   waits=[("PE", pe_po), ("HA" if oc % 2 == 0 else "HB", sy_h)], inc="DV")
                    po_free[oc % 2] = ("DV", dv_x1)
                    hso_free[oc % 2] = ("DV", dv_x1)
                    ac_sq = P.emit("scalar",
                                   lambda e, oc=oc: e.activation(
                                       sq_sb[oc % 2][:], x1_sb[:, oc * TOWN:(oc + 1) * TOWN],
                                       AF.Square),
                                   waits=[("DV", dv_x1), sq_free[oc % 2]], inc="AC")
                    if stat_pend is not None:
                        emit_stats(*stat_pend)
                    stat_pend = (oc, dv_x1, ac_sq)
                emit_stats(*stat_pend)

                def fin1(e):
                    e.tensor_scalar(m_sb[:], st_ps[:, 0:1024], 1.0 / D, None, ALU.mult)
                    e.tensor_scalar(ex2_sb[:], st_ps[:, 1024:2048], 1.0 / D, None, ALU.mult)
                    e.tensor_tensor(var_sb[:], m_sb[:], m_sb[:], ALU.mult)
                    return e.tensor_tensor(var_sb[:], ex2_sb[:], var_sb[:], ALU.subtract)
                dv_f1 = P.emit("vector", fin1, waits=[("PE", pe_stat)], inc="DV", nops=4)
                st_scope.close()
                lg_ps = e1(nc.psum_tensor("lg_ps", [8, TOWN], F32))
                h_ps = e1(nc.psum_tensor("h_ps", [128, TOWN], F32))
                ac_f1 = P.emit("scalar",
                               lambda e: e.activation(sd_sb[:], var_sb[:], AF.Sqrt, bias=eps_sb[0:1, 0:1]),
                               waits=[("DV", dv_f1)], inc="AC")
                def fin2(e):
                    e.reciprocal(rs_sb[:], sd_sb[:])
                    e.tensor_tensor(nmrs_sb[:], m_sb[:], rs_sb[:], ALU.mult)
                    return e.tensor_scalar(nmrs_sb[:], nmrs_sb[:], -1.0, None, ALU.mult)
                dv_f2 = P.emit("vector", fin2, waits=[("AC", ac_f1)], inc="DV", nops=3)
                def bcast(e):
                    last = None
                    for hf in (0, 1):
                        last = e.matmul(po[0][:, hf * 512:(hf + 1) * 512], onesf_sb[:],
                                        rs_sb[:, hf * 512:(hf + 1) * 512], start=True, stop=True)
                        last = e.matmul(po[1][:, hf * 512:(hf + 1) * 512], onesf_sb[:],
                                        nmrs_sb[:, hf * 512:(hf + 1) * 512], start=True, stop=True)
                    return last
                pe_bc = P.emit("tensor", bcast, waits=[("DV", dv_f2), ("DV", dv_x1)], inc="PE")
                dv_ap = 0
                pe_rt = 0
                dv_ap_at = []
                for oc in range(32):
                    def apply1(e, oc=oc):
                        e.tensor_tensor(lt_sb[:], x1_sb[:, oc * TOWN:(oc + 1) * TOWN],
                                        po[1][:], ALU.add)
                        return e.tensor_tensor(x1_sb[:, oc * TOWN:(oc + 1) * TOWN],
                                               lt_sb[:], po[0][:], ALU.mult)
                    dv_ap = P.emit("vector", apply1,
                                   waits=[("PE", max(pe_bc, pe_stat))], inc="DV", nops=2)
                    dv_ap_at.append(dv_ap)
                    def rtl(e, oc=oc):
                        last = None
                        for hf in (0, 1):
                            last = e.matmul(lg_ps[:, hf * 512:(hf + 1) * 512],
                                            wr_sb[:, oc * 8:(oc + 1) * 8],
                                            x1_sb[:, oc * TOWN + hf * 512: oc * TOWN + hf * 512 + 512],
                                            start=(oc == 0), stop=(oc == 31), skip_group_check=True)
                        for hf in (0, 1):
                            last = e.matmul(h_ps[:, hf * 512:(hf + 1) * 512],
                                            wla_sb[:, oc * 128:(oc + 1) * 128],
                                            x1_sb[:, oc * TOWN + hf * 512: oc * TOWN + hf * 512 + 512],
                                            start=(oc == 0), stop=(oc == 31), skip_group_check=True)
                        return last
                    pe_rt = P.emit("tensor", rtl,
                                   waits=[("DV", dv_ap), ("GS", gp_small)], inc="PE")
                dv_lg = P.emit("vector", lambda e: e.tensor_copy(lg_sb[:], lg_ps[:]),
                               waits=[("PE", pe_rt)], inc="DV")
                dv_h = P.emit("vector", lambda e: e.tensor_copy(h_sb[:], h_ps[:]),
                              waits=[("PE", pe_rt)], inc="DV")

            hw_sb = ec(nc.sbuf_tensor("hw_sb", [128, TOWN], BF16))
            lgt_sb = ec(nc.sbuf_tensor("lgt_sb", [128, 64], F32))
            oh1_sb = ec(nc.sbuf_tensor("oh1_sb", [128, 64], F32))
            oh2_sb = ec(nc.sbuf_tensor("oh2_sb", [128, 64], F32))
            l2_sb = ec(nc.sbuf_tensor("l2_sb", [128, 64], F32))
            wtok_sb = ec(nc.sbuf_tensor("wtok_sb", [128, 64], F32))
            wtk2_sb = ec(nc.sbuf_tensor("wtk2_sb", [128, 8], F32))
            tkw_sb = ec(nc.sbuf_tensor("tkw_sb", [128, 8], F32))
            tk2_sb = ec(nc.sbuf_tensor("tk2_sb", [128, 32], F32))
            wf_sb = ec(nc.sbuf_tensor("wf_sb", [8, TOWN], F32))
            x2tok_sb = ec(nc.sbuf_tensor("x2tok_sb", [128, 8 * D], BF16))
            x2c_sb = [ec(nc.sbuf_tensor(f"x2c{i}", [128, TOWN], F32)) for i in range(2)]
            s1p_sb = ec(nc.sbuf_tensor("s1p_sb", [128, 8 * 32], F32))
            s2p_sb = ec(nc.sbuf_tensor("s2p_sb", [128, 8 * 32], F32))
            s1_sb = ec(nc.sbuf_tensor("s1_sb", [128, 8], F32))
            s2_sb = ec(nc.sbuf_tensor("s2_sb", [128, 8], F32))
            mean2_sb = ec(nc.sbuf_tensor("mean2_sb", [128, 8], F32))
            var2_sb = ec(nc.sbuf_tensor("var2_sb", [128, 8], F32))
            rs2_sb = ec(nc.sbuf_tensor("rs2_sb", [128, 8], F32))
            nm2_sb = ec(nc.sbuf_tensor("nm2_sb", [128, 8], F32))
            ofin_sb = ec(nc.sbuf_tensor("ofin_sb", [128, D], F32))

            with ExitStack() as c2a:
                e2a = c2a.enter_context
                t8_ps = e2a(nc.psum_tensor("t8_ps", [128, 64], F32))
                wf_ps = e2a(nc.psum_tensor("wf_ps", [8, TOWN], F32))
                wexp_ps = e2a(nc.psum_tensor("wexp_ps", [128, TOWN], F32))

                def t8s(e):
                    last = None
                    for tt in range(8):
                        last = e.transpose(t8_ps[:, tt * 8:(tt + 1) * 8],
                                           lg_sb[:, tt * 128:(tt + 1) * 128], idf_sb[0:8, 0:8])
                    return last
                pe_t8 = P.emit("tensor", t8s, waits=[("DV", dv_lg)], inc="PE")

                def topk1(e):
                    last = None
                    for tt in range(8):
                        sl = lgt_sb[:, tt * 8:(tt + 1) * 8]
                        e.tensor_copy(sl, t8_ps[:, tt * 8:(tt + 1) * 8])
                        e.reduce_max(tk2_sb[:, tt:tt + 1], sl, axis=AX)
                        e.tensor_scalar(oh1_sb[:, tt * 8:(tt + 1) * 8], sl,
                                        tk2_sb[:, tt:tt + 1], None, ALU.is_equal)
                        e.scalar_tensor_tensor(l2_sb[:, tt * 8:(tt + 1) * 8],
                                               oh1_sb[:, tt * 8:(tt + 1) * 8], -1.0e30, sl,
                                               ALU.mult, ALU.add)
                        e.reduce_max(tk2_sb[:, 8 + tt:9 + tt], l2_sb[:, tt * 8:(tt + 1) * 8], axis=AX)
                        e.tensor_scalar(oh2_sb[:, tt * 8:(tt + 1) * 8],
                                        l2_sb[:, tt * 8:(tt + 1) * 8],
                                        tk2_sb[:, 8 + tt:9 + tt], None, ALU.is_equal)
                        last = e.tensor_tensor(tkw_sb[:, tt:tt + 1], tk2_sb[:, tt:tt + 1],
                                               tk2_sb[:, 8 + tt:9 + tt], ALU.subtract)
                    return last
                dv_t1 = P.emit("vector", topk1, waits=[("PE", pe_t8)], inc="DV", nops=56)
                ac_e = P.emit("scalar",
                              lambda e: e.activation(tk2_sb[:, 16:24], tkw_sb[:], AF.Exp,
                                                     scale=-1.0),
                              waits=[("DV", dv_t1)], inc="AC")
                def topk2(e):
                    last = None
                    for tt in range(8):
                        e.tensor_scalar(tk2_sb[:, 24 + tt:25 + tt], tk2_sb[:, 16 + tt:17 + tt],
                                        1.0, None, ALU.add)
                        e.reciprocal(tkw_sb[:, tt:tt + 1], tk2_sb[:, 24 + tt:25 + tt])
                        e.tensor_scalar(wtk2_sb[:, 0:1], tkw_sb[:, tt:tt + 1], 2.0, None, ALU.mult)
                        e.tensor_scalar(wtk2_sb[:, 1:2], tkw_sb[:, tt:tt + 1], -2.0, 2.0,
                                        ALU.mult, ALU.add)
                        e.tensor_scalar(wtok_sb[:, tt * 8:(tt + 1) * 8],
                                        oh1_sb[:, tt * 8:(tt + 1) * 8],
                                        wtk2_sb[:, 0:1], None, ALU.mult)
                        e.tensor_scalar(l2_sb[:, tt * 8:(tt + 1) * 8],
                                        oh2_sb[:, tt * 8:(tt + 1) * 8],
                                        wtk2_sb[:, 1:2], None, ALU.mult)
                        last = e.tensor_tensor(wtok_sb[:, tt * 8:(tt + 1) * 8],
                                               wtok_sb[:, tt * 8:(tt + 1) * 8],
                                               l2_sb[:, tt * 8:(tt + 1) * 8], ALU.add)
                    return last
                dv_t2 = P.emit("vector", topk2, waits=[("AC", ac_e)], inc="DV", nops=56)
                def wfs(e):
                    last = None
                    for tt in range(8):
                        last = e.transpose(wf_ps[:, tt * 128:(tt + 1) * 128],
                                           wtok_sb[:, tt * 8:(tt + 1) * 8], idf_sb[:])
                    return last
                pe_wf = P.emit("tensor", wfs, waits=[("DV", dv_t2)], inc="PE")
                dv_wf = P.emit("vector", lambda e: e.tensor_copy(wf_sb[:], wf_ps[:]),
                               waits=[("PE", pe_wf)], inc="DV")
                def wexp(e):
                    last = None
                    for hf in (0, 1):
                        last = e.matmul(wexp_ps[:, hf * 512:(hf + 1) * 512], e2_sb[:],
                                        wf_sb[:, hf * 512:(hf + 1) * 512], start=True, stop=True)
                    return last
                pe_we = P.emit("tensor", wexp, waits=[("DV", dv_wf)], inc="PE")
                dv_hw = P.emit("vector",
                               lambda e: e.tensor_tensor(hw_sb[:], h_sb[:], wexp_ps[:], ALU.mult),
                               waits=[("PE", pe_we), ("DV", dv_h)], inc="DV")

            with ExitStack() as c2b:
                e2b = c2b.enter_context
                y_ps = [e2b(nc.psum_tensor(f"y_ps{i}", [128, TOWN], F32)) for i in range(2)]
                tr_ps = [e2b(nc.psum_tensor(f"tr_ps{i}", [128, TOWN], F32)) for i in range(2)]

                y_free = [(None, 0), (None, 0)]
                x2c_free = [(None, 0), (None, 0)]
                tr_free = [(None, 0), (None, 0)]
                dv_tre = 0
                for oc in range(32):
                    wb = load_slab2(wbase[oc * 128:(oc + 1) * 128, :])
                    pe_y = 0
                    for hf in (0, 1):
                        for sub in range(4):
                            mms = [(wsl2[wb][:, kc * 128:(kc + 1) * 128],
                                    x1_sb[:, kc * TOWN + hf * 512: kc * TOWN + hf * 512 + 512])
                                   for kc in range(sub * 8, sub * 8 + 8)]
                            if sub == 3:
                                mms.append((wlb_sb[:, oc * 128:(oc + 1) * 128],
                                            hw_sb[:, hf * 512:(hf + 1) * 512]))
                            waits = [("DV", dv_ap_at[sub * 8 + 7])]
                            if hf == 0 and sub == 0:
                                waits += [("WA" if wb == 0 else "WB", slab2_ready[wb]),
                                          y_free[oc % 2]]
                            if sub == 3:
                                waits.append(("DV", dv_hw))
                            def fny(e, ms=list(mms), oc=oc, hf=hf, sub=sub):
                                last = None
                                for kx, (l, r) in enumerate(ms):
                                    last = e.matmul(y_ps[oc % 2][:, hf * 512:(hf + 1) * 512],
                                                    l, r, start=(sub == 0 and kx == 0),
                                                    stop=(sub == 3 and kx == len(ms) - 1),
                                                    skip_group_check=True)
                                return last
                            pe_y = P.emit("tensor", fny, waits, inc="PE")
                    slab2_lastuse[wb] = pe_y
                    dv_x2 = P.emit("vector",
                                   lambda e, oc=oc: e.scalar_tensor_tensor(
                                       x2c_sb[oc % 2][:], y_ps[oc % 2][:],
                                       bb_sb[:, oc:oc + 1],
                                       x1_sb[:, oc * TOWN:(oc + 1) * TOWN],
                                       ALU.add, ALU.add),
                                   waits=[("PE", pe_y), x2c_free[oc % 2]], inc="DV")
                    y_free[oc % 2] = ("DV", dv_x2)
                    def trs(e, oc=oc):
                        last = None
                        for tt in range(8):
                            last = e.transpose(tr_ps[oc % 2][:, tt * 128:(tt + 1) * 128],
                                               x2c_sb[oc % 2][:, tt * 128:(tt + 1) * 128],
                                               idf_sb[:])
                        return last
                    pe_tr = P.emit("tensor", trs,
                                   waits=[("DV", dv_x2), tr_free[oc % 2]], inc="PE")
                    dv_tre = P.emit("vector",
                                    lambda e, oc=oc: e.tensor_copy(
                                        x2tok_sb[:].rearrange("p (tt f) -> p tt f", tt=8)[:, :, oc * 128:(oc + 1) * 128],
                                        tr_ps[oc % 2][:].rearrange("p (tt f) -> p tt f", tt=8)),
                                    waits=[("PE", pe_tr)], inc="DV")
                    ac_sq2 = P.emit("scalar",
                                    lambda e, oc=oc: e.activation(
                                        x2c_sb[oc % 2][:], tr_ps[oc % 2][:], AF.Square),
                                    waits=[("PE", pe_tr), ("DV", dv_tre)], inc="AC")
                    def parts(e, oc=oc):
                        last = None
                        for tt in range(8):
                            e.reduce_sum(s1p_sb[:, tt * 32 + oc: tt * 32 + oc + 1],
                                         tr_ps[oc % 2][:, tt * 128:(tt + 1) * 128], axis=AX)
                            last = e.reduce_sum(s2p_sb[:, tt * 32 + oc: tt * 32 + oc + 1],
                                                x2c_sb[oc % 2][:, tt * 128:(tt + 1) * 128], axis=AX)
                        return last
                    dv_part = P.emit("vector", parts,
                                     waits=[("AC", ac_sq2)], inc="DV", nops=16)
                    x2c_free[oc % 2] = ("DV", dv_part)
                    tr_free[oc % 2] = [("DV", dv_part), ("AC", ac_sq2), ("DV", dv_tre)]

                sy_out = 0
                for tt in range(8):
                    def stats2(e, tt=tt):
                        e.reduce_sum(s1_sb[:, tt:tt + 1], s1p_sb[:, tt * 32:(tt + 1) * 32], axis=AX)
                        e.reduce_sum(s2_sb[:, tt:tt + 1], s2p_sb[:, tt * 32:(tt + 1) * 32], axis=AX)
                        e.tensor_scalar(mean2_sb[:, tt:tt + 1], s1_sb[:, tt:tt + 1], 1.0 / D, None, ALU.mult)
                        e.tensor_scalar(var2_sb[:, tt:tt + 1], s2_sb[:, tt:tt + 1], 1.0 / D, None, ALU.mult)
                        e.tensor_tensor(rs2_sb[:, tt:tt + 1], mean2_sb[:, tt:tt + 1],
                                        mean2_sb[:, tt:tt + 1], ALU.mult)
                        return e.tensor_tensor(var2_sb[:, tt:tt + 1], var2_sb[:, tt:tt + 1],
                                               rs2_sb[:, tt:tt + 1], ALU.subtract)
                    dv_s2 = P.emit("vector", stats2, waits=[("DV", dv_part)], inc="DV", nops=6)
                    ac_sd = P.emit("scalar",
                                   lambda e, tt=tt: e.activation(
                                       nm2_sb[:, tt:tt + 1], var2_sb[:, tt:tt + 1], AF.Sqrt, bias=eps_sb[:]),
                                   waits=[("DV", dv_s2)], inc="AC")
                    def fin3(e, tt=tt):
                        e.reciprocal(rs2_sb[:, tt:tt + 1], nm2_sb[:, tt:tt + 1])
                        e.tensor_tensor(nm2_sb[:, tt:tt + 1], mean2_sb[:, tt:tt + 1],
                                        rs2_sb[:, tt:tt + 1], ALU.mult)
                        return e.tensor_scalar(nm2_sb[:, tt:tt + 1], nm2_sb[:, tt:tt + 1],
                                               -1.0, None, ALU.mult)
                    dv_f3 = P.emit("vector", fin3, waits=[("AC", ac_sd)], inc="DV", nops=3)
                    dv_af = P.emit("vector",
                                   lambda e, tt=tt: e.tensor_scalar(
                                       ofin_sb[:], x2tok_sb[:, tt * D:(tt + 1) * D],
                                       rs2_sb[:, tt:tt + 1], nm2_sb[:, tt:tt + 1],
                                       ALU.mult, ALU.add),
                                   waits=[("DV", dv_f3), ("OT", sy_out)], inc="DV")
                    sy_out = P.dma("sync", "OT", out[tt * 128:(tt + 1) * 128, :], ofin_sb[:],
                                   waits=[("DV", dv_af)])

        P.emit("sync", lambda e: e.nop(), waits=[("OT", C["OT"])])

        class _ChainEng:
            def __init__(self, eng, sem):
                self._eng = eng
                self._sem = sem
                self.count = 0

            def __getattr__(self, name):
                f = getattr(self._eng, name)
                if name in ("wait_ge", "partition_id"):
                    return f

                def call(*a, **kw):
                    if self.count:
                        self._eng.wait_ge(self._sem, self.count)
                    inst = f(*a, **kw)
                    inst.then_inc(self._sem, 1)
                    self.count += 1
                    return inst
                return call

        with nc.Block() as block:
            def mkrep(name):
                def rep(eng):
                    maxw = {}
                    chain = name in ("vector", "scalar")
                    ceng = _ChainEng(eng, sems["DV" if name == "vector" else "AC"]) if chain else None
                    for waits, inc, n, fn in P.ops[name]:
                        for s, v in waits:
                            if v and v > maxw.get(s, 0):
                                eng.wait_ge(sems[s], v)
                                maxw[s] = v
                        if chain and isinstance(inc, tuple):
                            _, sem, pre, nops = inc
                            if pre and pre > maxw.get(sem, 0):
                                eng.wait_ge(sems[sem], pre)
                                maxw[sem] = pre
                            assert ceng.count == pre, (name, ceng.count, pre)
                            fn(ceng)
                            assert ceng.count == pre + nops, (name, ceng.count, pre, nops)
                        else:
                            inst = fn(eng)
                            if inc is not None and inst is not None:
                                inst.then_inc(sems[inc], n)
                return rep

            block.sync(mkrep("sync"))
            block.gpsimd(mkrep("gpsimd"))
            block.tensor(mkrep("tensor"))
            block.vector(mkrep("vector"))
            block.scalar(mkrep("scalar"))

    return nc


def _prep_lhsT(w_rows):
    f = w_rows.shape[0]
    t = w_rows.reshape(f // 128, 128, 32, 128).transpose(0, 3, 2, 1)
    return np.ascontiguousarray(t.reshape(f, 4096), dtype=np.float32)


def _swap_rows(w, half):
    idx = np.empty(512, dtype=np.int64)
    n = 0
    for sc in range(4):
        for m in range(128):
            h_loc = 2 * sc + m // 64
            r = m % 64
            idx[n] = half * 2048 + h_loc * 256 + (r ^ 1)
            n += 1
    return w[idx]


def kernel(L0_attn_hidden_states, L0_attn_w_q, L0_attn_w_k, L0_attn_w_v,
           L0_attn_w_out, L0_ffn_W_base, L0_ffn_b_base, L0_ffn_W_router,
           L0_ffn_lora_A, L0_ffn_lora_B):
    hs = np.asarray(L0_attn_hidden_states, dtype=np.float32)
    w_q = np.asarray(L0_attn_w_q, dtype=np.float32)
    w_k = np.asarray(L0_attn_w_k, dtype=np.float32)
    w_v = np.asarray(L0_attn_w_v, dtype=np.float32)
    w_out = np.asarray(L0_attn_w_out, dtype=np.float32)
    W_base = np.asarray(L0_ffn_W_base, dtype=np.float32)
    b_base = np.asarray(L0_ffn_b_base, dtype=np.float32)
    W_router = np.asarray(L0_ffn_W_router, dtype=np.float32)
    lora_A = np.asarray(L0_ffn_lora_A, dtype=np.float32)
    lora_B = np.asarray(L0_ffn_lora_B, dtype=np.float32)

    hst_b = [np.ascontiguousarray(hs[b].T) for b in range(4)]
    perm = np.arange(2048)
    for h in range(8):
        if h % 2 == 1:
            c0 = h * 256
            perm[c0:c0 + 128] = np.concatenate(
                [np.arange(c0 + 64, c0 + 128), np.arange(c0, c0 + 64)])
    wq_h = [_prep_lhsT(w_q[h * 2048:(h + 1) * 2048][perm]) for h in (0, 1)]
    wk_h = [_prep_lhsT(w_k[h * 2048:(h + 1) * 2048][perm]) for h in (0, 1)]
    wv_h = []
    for h in (0, 1):
        t = w_v[h * 2048:(h + 1) * 2048].reshape(8, 256, 32, 128).transpose(0, 3, 2, 1)
        wv_h.append(np.ascontiguousarray(t.reshape(1024, 8192), dtype=np.float32))
    wout_h = []
    for h in (0, 1):
        perm_io = np.concatenate([np.arange(h * 2048, (h + 1) * 2048),
                                  np.arange((1 - h) * 2048, (2 - h) * 2048)])
        wout_h.append(_prep_lhsT(np.ascontiguousarray(w_out[:, perm_io])))
    wbase_p = _prep_lhsT(W_base)
    bb_p = np.ascontiguousarray(b_base.reshape(32, 128).T)
    wr_p = np.ascontiguousarray(
        W_router.T.reshape(32, 128, 8).transpose(1, 0, 2).reshape(128, 256))
    a_cat = lora_A.reshape(128, 4096)
    wla_p = np.ascontiguousarray(
        a_cat.T.reshape(32, 128, 128).transpose(1, 0, 2).reshape(128, 4096))
    wlb_p = np.ascontiguousarray(lora_B.transpose(0, 2, 1).reshape(128, 4096))

    inv = (1.0 / np.float32(10000.0) ** (np.arange(0, 64, 2, dtype=np.float32)
                                         / np.float32(64.0))).astype(np.float32)
    ang = (np.arange(S, dtype=np.float32)[None, :] * inv[:, None]).astype(np.float32)
    cosj = np.cos(ang).astype(np.float32)
    sinj = np.sin(ang).astype(np.float32)
    cos2 = np.empty((128, S), np.float32)
    sin2 = np.empty((128, S), np.float32)
    cos2[0:64:2] = cosj; cos2[1:64:2] = cosj
    sin2[0:64:2] = -sinj; sin2[1:64:2] = sinj
    cos2[64:128] = cos2[0:64]
    sin2[64:128] = sin2[0:64]
    identb = np.eye(128, dtype=np.float32).astype(bf16)
    identf = np.eye(128, dtype=np.float32)
    q_i = np.arange(128)[:, None]
    k_i = np.arange(128)[None, :]
    maskt = np.where(k_i <= q_i, 0.0, NEG).astype(np.float32)
    masktb = maskt.astype(bf16)
    onesb = np.ones((128, 1), np.float32).astype(bf16)
    onesf = np.ones((1, 128), np.float32)
    e2f = np.zeros((8, 128), np.float32)
    epsv_h = np.full((128, 1), 1.0e-5, dtype=np.float32)
    for e in range(8):
        e2f[e, e * 16:(e + 1) * 16] = 1.0

    nc = build_nc()
    in_maps = []
    for c in range(8):
        b, half = c // 2, c % 2
        in_maps.append({
            "hst": hst_b[b],
            "hsto": np.ascontiguousarray(hst_b[b][:, half * TOWN:(half + 1) * TOWN]),
            "wq": wq_h[half], "wk": wk_h[half],
            "wv": wv_h[half],
            "wout": wout_h[half], "wbase": wbase_p, "bb": bb_p,
            "wr": wr_p, "wla": wla_p, "wlb": wlb_p,
            "e2f": e2f, "cosd": cos2, "sind": sin2,
            "identb": identb, "identf": identf, "masktb": masktb,
            "onesb": onesb, "onesf": onesf, "epsv": epsv_h,
        })
    res = run_bass_kernel_spmd(nc, in_maps, core_ids=list(range(8)))
    outp = np.empty((4, S, D), dtype=np.float32)
    for c in range(8):
        b, half = c // 2, c % 2
        outp[b, half * TOWN:(half + 1) * TOWN, :] = res.results[c]["out"]
    return outp

